# version 17
# baseline (speedup 1.0000x reference)
import contextlib
import math
import numpy as np
import ml_dtypes

import concourse.bass as bass
import concourse.mybir as mybir
from concourse.bass_utils import run_bass_kernel_spmd

F32 = mybir.dt.float32
BF16 = mybir.dt.bfloat16
AF = mybir.ActivationFunctionType
ALU = mybir.AluOpType
AX = mybir.AxisListType.X

D = 4096
HEADS = 32
KV = 8
HD = 128
GQ = HEADS // KV
NE = 4
LR = 16
ER = NE * LR
SCALE_FFN = 4.0
SM_SCALE = 1.0 / math.sqrt(HD)
NEG = -1.0e30
EPS_LN = 1e-5
EPS_LA = 1e-10
DT = D // 128

ENGINES = ("tensor", "vector", "scalar", "gpsimd", "sync")


class Prog:

    def __init__(self, nc, stack):
        self.nc = nc
        self.ops = {e: [] for e in ENGINES}
        self.sem_handles = {}
        self.counts = {}
        self._stack = stack

    def sem(self, name):
        if name not in self.sem_handles:
            self.sem_handles[name] = self._stack.enter_context(self.nc.semaphore(name))
            self.counts[name] = 0
        return self.sem_handles[name]

    def inc(self, name, n=1):
        self.sem(name)
        self.counts[name] += n
        return self.counts[name]

    def cur(self, name):
        self.sem(name)
        return self.counts[name]

    def op(self, eng, meth, *args, inc=None, **kw):
        incs = []
        outc = []
        if inc:
            if isinstance(inc, tuple):
                inc = [inc]
            for (s, n) in inc:
                c = self.inc(s, n)
                incs.append((self.sem_handles[s], n))
                outc.append(c)
        self.ops[eng].append((meth, args, kw, incs[:1]))
        for extra in incs[1:]:
            self.ops[eng].append(("nop", (), {}, [extra]))
        return outc[0] if len(outc) == 1 else outc

    def wait(self, eng, sem_name, value):
        if value <= 0:
            return
        h = self.sem(sem_name)
        self.ops[eng].append(("wait_ge", (h, value), {}, []))

    def drain(self, eng):
        self.ops[eng].append(("drain", (), {}, []))

    def raw(self, eng, fn):
        self.ops[eng].append(("__raw__", (fn,), {}, []))

    def dma(self, eng, out, in_, sem_name, **kw):
        prev = self.cur(sem_name)
        if prev > 0:
            self.wait(eng, sem_name, prev)
        return self.op(eng, "dma_start", inc=(sem_name, 16), out=out, in_=in_, **kw)

    def barrier(self):
        self.sem("bar")
        for e in ENGINES:
            self.drain(e)
            self.op(e, "nop", inc=("bar", 1))
        tgt = self.cur("bar")
        for e in ENGINES:
            self.wait(e, "bar", tgt)

    def finalize(self, block):
        for e in ENGINES:
            ops = self.ops[e]

            def make(ops=ops):
                def run(eng):
                    for (meth, args, kw, incs) in ops:
                        if meth == "__raw__":
                            args[0](eng)
                            continue
                        inst = getattr(eng, meth)(*args, **kw)
                        for (h, n) in incs:
                            inst.then_inc(h, n)
                return run
            getattr(block, e)(make())


def build(nt=16, debug=None, n_pairs=4):
    debug = debug or {}
    TLOC = nt * 128
    TC = TLOC // 512
    NTG = nt // 4
    NG = TLOC // 1024
    nc = bass.Bass("TRN2", target_bir_lowering=False)

    def din(name, shape, dt=F32):
        return nc.dram_tensor(name, shape, dt, kind="ExternalInput").ap()

    def dint(name, shape, dt=F32):
        return nc.dram_tensor(name, shape, dt, kind="Internal").ap()

    def dout(name, shape, dt=F32):
        return nc.dram_tensor(name, shape, dt, kind="ExternalOutput").ap()

    x0 = din("x0", [TLOC, D])
    wqT = din("wqT", [D, D])
    wkT = din("wkT", [D, KV * HD])
    wvT = din("wvT", [D, KV * HD])
    woT = din("woT", [D, D])
    wqkv = din("wqkv", [D, D + 2 * KV * HD])
    wo2 = din("wo2", [D, D])
    awr = din("awr", [DT, 128, ER + NE])
    ballT = din("ballT", [ER, D])
    bqT = din("bqT", [128, DT])
    bkT = din("bkT", [128, KV])
    bvT = din("bvT", [128, KV])
    bo_bc = din("bo_bc", [128, D])
    identb = din("identb", [128, 128], BF16)
    ident32 = din("ident32", [128, 128], F32)
    cosf = din("cosf", [64, TLOC])
    sinf = din("sinf", [64, TLOC])
    maskd = din("maskd", [2, 128, 512])
    e4 = din("e4", [NE, ER], BF16)
    onesb = din("onesb", [128, 1], BF16)
    fconst = din("fconst", [128, 8])
    FCONST_VALS = [EPS_LN, -0.5, -1.0, SM_SCALE, 1.0, 0.0, SCALE_FFN, EPS_LA]

    x1d = dint("x1d", [TLOC, D])
    x2d = dint("x2d", [TLOC, D])
    x3d = dint("x3d", [TLOC, D])
    qTb = dint("qTb", [D, TLOC], BF16)
    afa = dint("afa", [D, TLOC], BF16)
    afb = dint("afb", [D, TLOC], BF16)
    kx = dint("kx", [KV * HD, TLOC], BF16)
    vx = dint("vx", [TLOC, KV * HD], BF16)
    stl = dint("stl", [KV, 128, 129])
    I32 = mybir.dt.int32
    rk = din("rk", [1, 8], I32)
    tokin = din("tokin", [1, 8], I32)
    sh_st_h = nc.dram_tensor("sh_st", [2 * KV * 128, 129], F32, kind="Internal",
                             addr_space="Shared")
    sh_kx_h = nc.dram_tensor("sh_kx", [2 * KV * HD, TLOC], BF16, kind="Internal",
                             addr_space="Shared")
    sh_vx_h = nc.dram_tensor("sh_vx", [2 * TLOC, KV * HD], BF16, kind="Internal",
                             addr_space="Shared")
    sh_fl_h = nc.dram_tensor("sh_fl", [4, 8], I32, kind="Internal",
                             addr_space="Shared")
    kxg = sh_kx_h.ap()
    vxg = sh_vx_h.ap()
    sh_st = sh_st_h.ap()
    sh_fl = sh_fl_h.ap()
    out = dout("out", [TLOC, D])

    dbg_t = {}
    for name, (shape, dt) in debug.items():
        dbg_t[name] = dout("dbg_" + name, shape, dt)

    RG = [[2 * i, 2 * i + 1] for i in range(n_pairs)]

    stack = contextlib.ExitStack()
    with stack:
        P = Prog(nc, stack)

        def sb(name, shape, dt, st=stack):
            return st.enter_context(nc.sbuf_tensor(name, shape, dt))

        def ps(name, shape, dt, st):
            return st.enter_context(nc.psum_tensor(name, shape, dt))

        yT = sb("yT", [128, DT * TLOC], BF16)
        ident_sb = sb("ident_sb", [128, 128], BF16)
        id32_sb = sb("id32_sb", [128, 128], F32)
        AWB = {}
        e4_sb = sb("e4_sb", [NE, ER], BF16)
        ones_sb = sb("ones_sb", [128, 1], BF16)
        bq_sb = sb("bq_sb", [128, DT], F32)
        bk_sb = sb("bk_sb", [128, KV], F32)
        bv_sb = sb("bv_sb", [128, KV], F32)
        fc_sb = sb("fc_sb", [128, 8], F32)
        rks_sb = sb("rks_sb", [1, 8], mybir.dt.int32)
        toks_sb = sb("toks_sb", [1, 8], mybir.dt.int32)
        for i, v in enumerate(FCONST_VALS):
            nc.const_aps.register(F32, v, fc_sb[:, i:i + 1])
        bns = [sb(f"bns{i}", [128, 48], F32) for i in range(2)]
        mv = [sb(f"mv{i}", [128, 2], F32) for i in range(2)]
        rstd = [sb(f"rstd{i}", [128, 1], F32) for i in range(2)]
        nmr = [sb(f"nmr{i}", [128, 1], F32) for i in range(2)]
        WB = {"b": None}
        XB = {"xt": None, "ybf": None}

        TP = {"banks": None}
        PJ = {"banks": None}

        def transpose_pack(src_slices, dst_ap_fn):
            n8 = len(src_slices)
            bank = P.cur("tpf") % 2
            if P.cur("tpf") >= 2:
                P.wait("tensor", "tpe", P.cur("tpf") - 1)
            tp = TP["banks"][bank]
            for j, s in enumerate(src_slices):
                P.op("tensor", "transpose", tp[:, j * 128:(j + 1) * 128], s, ident_sb[:],
                     inc=("tpf", 1) if j == n8 - 1 else None)
            P.wait("vector", "tpf", P.cur("tpf"))
            P.op("vector", "tensor_copy", dst_ap_fn(n8),
                 tp[:].rearrange("p (a f) -> p a f", f=128)[:, 0:n8, :],
                 inc=("tpe", 1))
            return P.cur("tpe")

        def transpose_tile_to_yt(src_bf, tile_k):
            for pk in range(4):
                srcs = [src_bf[:, (pk * 8 + j) * 128:(pk * 8 + j + 1) * 128] for j in range(8)]

                def dst(n8, pk=pk):
                    return yT[:].rearrange("p (a t) -> p a t", a=DT)[
                        :, pk * 8:pk * 8 + n8, tile_k * 128:(tile_k + 1) * 128]
                transpose_pack(srcs, dst)
            return P.cur("tpe")

        def tok_transpose(src_bf, dst_sb):
            for pk in range((nt + 7) // 8):
                n8 = min(8, nt - pk * 8)
                srcs = [src_bf[:, (pk * 8 + j) * 128:(pk * 8 + j + 1) * 128] for j in range(n8)]

                def dst(n, pk=pk):
                    return dst_sb[:, pk * 8 * 128:(pk * 8 + n) * 128]
                transpose_pack(srcs, dst)
            return P.cur("tpe")

        def ln_tile(xt_ap, b):
            nch = D // 512
            for j in range(nch):
                P.op("vector", "bn_stats", bns[b][:, j * 6:(j + 1) * 6],
                     xt_ap[:, j * 512:(j + 1) * 512])
            P.drain("vector")
            P.op("vector", "bn_aggr", mv[b][:], bns[b][:, 0:6 * nch], inc=("lnv", 1))
            vv = P.cur("lnv")
            P.wait("scalar", "lnv", vv)
            P.op("scalar", "activation", rstd[b][:], mv[b][:, 1:2], AF.Ln, bias=EPS_LN)
            P.drain("scalar")
            P.op("scalar", "activation", rstd[b][:], rstd[b][:], AF.Exp, scale=-0.5,
                 inc=("lns", 1))
            sv = P.cur("lns")
            P.wait("vector", "lns", sv)
            P.op("vector", "scalar_tensor_tensor", nmr[b][:], mv[b][:, 0:1], -1.0, rstd[b][:],
                 ALU.mult, ALU.mult, inc=("lnv", 1))
            return P.cur("lnv")

        def proj_block(wsrc, col0, nbank_tc, yt_wait=None):
            wblk = WB["b"]
            nproj = P.cur("wl")
            bw = nproj % 2
            P.inc("wl")
            if nproj >= 2:
                P.wait("gpsimd", "pjf", (nproj - 1) * TC)
            wv = P.dma("gpsimd", wblk[bw][:].rearrange("p (a f) -> p a f", a=DT),
                       wsrc[:, col0:col0 + 128].rearrange("(a p) f -> p a f", p=128), "wd")
            P.wait("tensor", "wd", wv)
            if yt_wait is not None:
                P.wait("tensor", yt_wait[0], yt_wait[1])
            prj = PJ["banks"]
            for tc in range(TC):
                if nproj >= 1:
                    P.wait("tensor", "pje", (nproj - 1) * TC + tc + 1)
                for dt_i in range(DT):
                    P.op("tensor", "matmul", prj[tc][:],
                         wblk[bw][:, dt_i * 128:(dt_i + 1) * 128],
                         yT[:, dt_i * TLOC + tc * 512:dt_i * TLOC + (tc + 1) * 512],
                         start=(dt_i == 0), stop=(dt_i == DT - 1),
                         inc=(("pjf", 1) if dt_i == DT - 1 else None))

        def mk_copy_to_shared(dst_ap_by_rank, src_ap, flag_row_base, wait_vals):
            v_copy, v_flag = wait_vals

            def fn(g):
                rr = g.alloc_register()
                dsh = P.sem_handles["dsh"]
                g.reg_load(rr, rks_sb[0:1, 0:1])
                with g.If_eq(rr, 0):
                    g.dma_start(out=dst_ap_by_rank[0], in_=src_ap).then_inc(dsh, 16)
                with g.Else():
                    g.dma_start(out=dst_ap_by_rank[1], in_=src_ap).then_inc(dsh, 16)
                g.wait_ge(dsh, v_copy)
                with g.If_eq(rr, 0):
                    g.dma_start(out=sh_fl[flag_row_base:flag_row_base + 1, :],
                                in_=toks_sb[0:1, :]).then_inc(dsh, 16)
                with g.Else():
                    g.dma_start(out=sh_fl[flag_row_base + 1:flag_row_base + 2, :],
                                in_=toks_sb[0:1, :]).then_inc(dsh, 16)
                g.wait_ge(dsh, v_flag)
            return fn

        def mk_poll(flag_row_base):
            def fn(sy):
                tkr = sy.alloc_register()
                pm0 = sy.alloc_register()
                pm1 = sy.alloc_register()
                pc = sy.alloc_register()
                pc2 = sy.alloc_register()
                sy.reg_load(tkr, toks_sb[0:1, 0:1])

                def check():
                    sy.reg_load(pm0, sh_fl[flag_row_base:flag_row_base + 1, 0:1])
                    sy.reg_load(pm1, sh_fl[flag_row_base + 1:flag_row_base + 2, 0:1])
                    sy.reg_alu(pc, pm0, tkr, ALU.bitwise_xor)
                    sy.reg_alu(pc2, pm1, tkr, ALU.bitwise_xor)
                    sy.reg_alu(pc, pc, pc2, ALU.bitwise_or)
                check()
                with sy.While(pc):
                    sy.nop(cycle_cnt=4096)
                    check()
            return fn

        with nc.Block() as block:
            P.dma("sync", ident_sb[:], identb, "dC")
            P.dma("sync", id32_sb[:], ident32, "dC")
            P.dma("sync", fc_sb[:], fconst, "dC")
            P.dma("sync", rks_sb[:], rk, "dC")
            P.dma("sync", toks_sb[:], tokin, "dC")
            P.dma("sync", bq_sb[:], bqT, "dC")
            P.dma("sync", bk_sb[:], bkT, "dC")
            P.dma("sync", bv_sb[:], bvT, "dC")
            P.dma("gpsimd", e4_sb[:], e4, "dCg")
            P.dma("gpsimd", ones_sb[:], onesb, "dCg")
            cAll = P.cur("dC")
            cAllg = P.cur("dCg")
            for e in ENGINES:
                P.wait(e, "dC", cAll)
                P.wait(e, "dCg", cAllg)

            with contextlib.ExitStack() as ph:
                tp1 = contextlib.ExitStack()
                p1buf = contextlib.ExitStack()
                XB["xt"] = [sb(f"xtP{i}", [128, D], F32, p1buf) for i in range(2)]
                XB["ybf"] = [sb(f"ybfP{i}", [128, D], BF16, p1buf) for i in range(2)]
                TP["banks"] = [ps(f"tpP1_{i}", [128, 1024], BF16, tp1) for i in range(2)]

                ytv_hist = []
                def ln_to_yt(src_dram, k, tag):
                    xt = XB["xt"]; ybf = XB["ybf"]
                    b = k % 2
                    if k >= 2:
                        P.wait("sync", f"nrm{tag}", k - 1)
                    lv = P.dma("sync", xt[b][:], src_dram[k * 128:(k + 1) * 128, :], f"dx{tag}")
                    P.wait("vector", f"dx{tag}", lv)
                    if k >= 2:
                        P.wait("vector", f"nrm{tag}", k - 1)
                    ln_tile(xt[b], b)
                    vv = P.cur("lnv")
                    P.wait("scalar", "lnv", vv)
                    if k >= 2:
                        P.wait("scalar", "tpe", ytv_hist[k - 2])
                    P.op("scalar", "activation", ybf[b][:], xt[b][:], AF.Identity,
                         bias=nmr[b][:, 0:1], scale=rstd[b][:, 0:1], inc=(f"nrm{tag}", 1))
                    P.wait("tensor", f"nrm{tag}", k + 1)
                    ytv_hist.append(transpose_tile_to_yt(ybf[b], k))

                for k in range(nt):
                    ln_to_yt(x0, k, "P1")
                yt1_done = ("tpe", P.cur("tpe"))

                P.barrier()
                p1buf.close()

                with contextlib.ExitStack() as pha:
                    WB["b"] = [sb(f"wblkA{i}", [128, DT * 128], BF16, pha) for i in range(2)]
                    PJ["banks"] = [ps(f"pjA_{i}", [128, 512], F32, pha) for i in range(TC)]
                    sk_ps = ps("sk_ps", [128, 132], F32, pha)
                    kft = sb("kft", [128, TLOC], BF16, pha)
                    vft = sb("vft", [128, TLOC], BF16, pha)
                    ktok = sb("ktok", [128, TLOC], BF16, pha)
                    vtok = sb("vtok", [128, TLOC], BF16, pha)
                    tmp1 = sb("tmp1", [128, 512], F32, pha)
                    tmp2 = sb("tmp2", [128, 512], F32, pha)
                    stage = sb("stage", [128, KV * 129], F32, pha)

                    def phi_evac(dst_bf, bias_col):
                        prj = PJ["banks"]
                        base = P.cur("pjf") - TC
                        for tc in range(TC):
                            sl = slice(tc * 512, (tc + 1) * 512)
                            P.wait("vector", "pjf", base + tc + 1)
                            P.drain("vector")
                            P.op("vector", "tensor_scalar", tmp1[:], prj[tc][:], bias_col, 0.0,
                                 ALU.add, ALU.min)
                            P.op("vector", "tensor_scalar", tmp2[:], prj[tc][:], bias_col, 0.0,
                                 ALU.add, ALU.max, inc=("phv", 1))
                            P.wait("scalar", "phv", P.cur("phv"))
                            P.op("scalar", "activation", tmp1[:], tmp1[:], AF.Exp, inc=("phs", 1))
                            P.wait("vector", "phs", P.cur("phs"))
                            P.op("vector", "tensor_tensor", dst_bf[:, sl], tmp2[:], tmp1[:],
                                 ALU.add, inc=("pje", 1))

                    def plain_evac(dst_bf, bias_col):
                        prj = PJ["banks"]
                        base = P.cur("pjf") - TC
                        for tc in range(TC):
                            sl = slice(tc * 512, (tc + 1) * 512)
                            P.wait("vector", "pjf", base + tc + 1)
                            if bias_col is None:
                                P.op("vector", "tensor_copy", dst_bf[:, sl], prj[tc][:],
                                     inc=("pje", 1))
                            else:
                                P.op("vector", "tensor_scalar", dst_bf[:, sl], prj[tc][:],
                                     bias_col, None, ALU.add, inc=("pje", 1))

                    ktv = vtv = 0
                    for g in range(KV):
                        if g > 0:
                            P.wait("vector", "tpe", ktv)
                        proj_block(wkT, g * 128, TC, yt_wait=yt1_done if g == 0 else None)
                        phi_evac(kft, bk_sb[:, g:g + 1])
                        P.wait("tensor", "pje", P.cur("pje"))
                        ktv = tok_transpose(kft, ktok)
                        if g > 0:
                            P.wait("vector", "tpe", vtv)
                        proj_block(wvT, g * 128, TC)
                        plain_evac(vft, bv_sb[:, g:g + 1])
                        P.wait("tensor", "pje", P.cur("pje"))
                        vtv = tok_transpose(vft, vtok)
                        P.wait("tensor", "tpe", P.cur("tpe"))
                        if g > 0:
                            P.wait("tensor", "ste", g)
                        for tt in range(nt):
                            P.op("tensor", "matmul", sk_ps[:, 0:128],
                                 ktok[:, tt * 128:(tt + 1) * 128],
                                 vtok[:, tt * 128:(tt + 1) * 128],
                                 start=(tt == 0), stop=(tt == nt - 1))
                        for tt in range(nt):
                            P.op("tensor", "matmul", sk_ps[:, 128:129],
                                 ktok[:, tt * 128:(tt + 1) * 128],
                                 ones_sb[:], start=(tt == 0), stop=(tt == nt - 1),
                                 inc=([("stf", 1)] if tt == nt - 1 else None))
                        P.wait("vector", "stf", P.cur("stf"))
                        P.op("vector", "tensor_copy", stage[:, g * 129:g * 129 + 129],
                             sk_ps[:, 0:129], inc=("ste", 1))
                    P.wait("sync", "ste", KV)
                    P.dma("sync", stl.rearrange("g p c -> p g c"),
                          stage[:].rearrange("p (g c) -> p g c", g=KV), "dstl")

                    P.wait("gpsimd", "dstl", P.cur("dstl"))
                    P.sem("dsh")
                    wvA = (P.inc("dsh", 16), P.inc("dsh", 16))
                    P.raw("gpsimd", mk_copy_to_shared(
                        [sh_st[0:KV * 128, :], sh_st[KV * 128:2 * KV * 128, :]],
                        stl.rearrange("g p c -> (g p) c"), 0, wvA))

                P.barrier()
                tp1.close()

                with contextlib.ExitStack() as pha:
                    WB["b"] = [sb(f"wblkQ{i}", [128, DT * 128], BF16, pha) for i in range(2)]
                    PJ["banks"] = [ps(f"pjQ_{i}", [128, 512], F32, pha) for i in range(TC)]
                    at_ps = [ps(f"at_{i}", [128, 512], F32, pha) for i in range(2)]
                    phiq = sb("phiq", [128, TLOC], BF16, pha)
                    afst = sb("afst", [128, TLOC], BF16, pha)
                    tmp1 = sb("tmp1a", [128, 512], F32, pha)
                    tmp2 = sb("tmp2a", [128, 512], F32, pha)
                    state_sb = sb("state_sb", [128, KV * 129], F32, pha)
                    st_bf = sb("st_bf", [128, KV * 128], BF16, pha)
                    rec_sb = sb("rec_sb", [128, KV], F32, pha)

                    state_b2 = sb("state_b2", [128, KV * 129], F32, pha)
                    P.raw("sync", mk_poll(0))
                    P.dma("sync", state_sb[:].rearrange("p (g c) -> p g c", g=KV),
                          sh_st[0:KV * 128, :].rearrange("(g p) c -> p g c", p=128),
                          "dstr")
                    sv = P.dma("sync", state_b2[:].rearrange("p (g c) -> p g c", g=KV),
                               sh_st[KV * 128:2 * KV * 128, :]
                               .rearrange("(g p) c -> p g c", p=128), "dstr")
                    P.wait("vector", "dstr", sv)
                    P.op("vector", "tensor_tensor", state_sb[:], state_sb[:],
                         state_b2[:], ALU.add)
                    P.drain("vector")
                    stv = state_sb[:].rearrange("p (g c) -> p g c", g=KV)
                    P.op("vector", "tensor_copy",
                         st_bf[:].rearrange("p (g c) -> p g c", g=KV), stv[:, :, 0:128])
                    P.op("vector", "tensor_scalar",
                         rec_sb[:].rearrange("p (g o) -> p g o", o=1), stv[:, :, 128:129],
                         EPS_LA, None, ALU.add)
                    P.drain("vector")
                    P.op("vector", "reciprocal", rec_sb[:], rec_sb[:], inc=("strdy", 1))
                    if "state" in dbg_t:
                        P.wait("sync", "strdy", 1)
                        P.dma("sync", dbg_t["state"], state_sb[:], "ddbg")

                    P.wait("tensor", "strdy", 1)

                    def phi_evac_q(dst_bf, bias_col):
                        prj = PJ["banks"]
                        base = P.cur("pjf") - TC
                        for tc in range(TC):
                            sl = slice(tc * 512, (tc + 1) * 512)
                            P.wait("vector", "pjf", base + tc + 1)
                            P.drain("vector")
                            P.op("vector", "tensor_scalar", tmp1[:], prj[tc][:], bias_col, 0.0,
                                 ALU.add, ALU.min)
                            P.op("vector", "tensor_scalar", tmp2[:], prj[tc][:], bias_col, 0.0,
                                 ALU.add, ALU.max, inc=("phv", 1))
                            P.wait("scalar", "phv", P.cur("phv"))
                            P.op("scalar", "activation", tmp1[:], tmp1[:], AF.Exp, inc=("phs", 1))
                            P.wait("vector", "phs", P.cur("phs"))
                            P.op("vector", "tensor_tensor", dst_bf[:, sl], tmp2[:], tmp1[:],
                                 ALU.add, inc=("pje", 1))

                    for h in range(HEADS):
                        g = h // GQ
                        if h > 0:
                            P.wait("vector", "atf", TC * h)
                        proj_block(wqT, h * 128, TC)
                        phi_evac_q(phiq, bq_sb[:, h:h + 1])
                        P.wait("tensor", "pje", P.cur("pje"))
                        if h > 0:
                            P.wait("vector", "dafa", 16 * h)
                        for tc in range(TC):
                            bb = tc % 2
                            n_at = TC * h + tc
                            if n_at >= 2:
                                P.wait("tensor", "ate", n_at - 1)
                            P.op("tensor", "matmul", at_ps[bb][:],
                                 st_bf[:, g * 128:(g + 1) * 128],
                                 phiq[:, tc * 512:(tc + 1) * 512],
                                 start=True, stop=True, inc=("atf", 1))
                            P.wait("vector", "atf", P.cur("atf"))
                            P.op("vector", "tensor_scalar", afst[:, tc * 512:(tc + 1) * 512],
                                 at_ps[bb][:], rec_sb[:, g:g + 1], None, ALU.mult,
                                 inc=("ate", 1))
                        P.wait("sync", "ate", P.cur("ate"))
                        P.dma("sync", afa[h * 128:(h + 1) * 128, :], afst[:], "dafa")
                    P.wait("sync", "dafa", P.cur("dafa"))

                P.barrier()

                def out_proj(wsrc, af_in_yt, src_x, dst_x, add_bo, tag):
                    GS = min(8, nt)
                    NGrp = nt // GS
                    with contextlib.ExitStack() as pho:
                        ob = [ps(f"ob{tag}_{i}", [128, 512], F32, pho) for i in range(GS)]
                        wr2 = [sb(f"wr2{tag}_{i}", [128, DT * 256], BF16, pho) for i in range(2)]
                        xl = sb(f"xl{tag}", [128, 8 * 512], F32, pho)
                        xs = sb(f"xs{tag}", [128, 8 * 512], F32, pho)
                        bo_st = sb(f"bo{tag}", [128, 512], F32, pho) if add_bo else None
                        for grp in range(NGrp):
                            for fch in range(8):
                                it = grp * 8 + fch
                                if add_bo:
                                    P.wait("sync", f"pe{tag}", it)
                                    P.dma("sync", bo_st[:], bo_bc[:, fch * 512:(fch + 1) * 512],
                                          f"dbo{tag}")
                                P.wait("sync", f"pe{tag}", it)
                                P.dma("sync", xl[:, 0:GS * 512].rearrange("p (a f) -> p a f", a=GS),
                                      src_x.rearrange("(a p) d -> p a d", p=128)
                                      [:, grp * GS:(grp + 1) * GS, fch * 512:(fch + 1) * 512],
                                      f"dxl{tag}")
                                for half in range(2):
                                    nw = it * 2 + half
                                    bw = nw % 2
                                    if it >= 1:
                                        P.wait("gpsimd", f"wh{half}{tag}", it)
                                    P.dma("gpsimd", wr2[bw][:].rearrange("p (a f) -> p a f", a=DT),
                                          wsrc[:, fch * 512 + half * 256:
                                               fch * 512 + (half + 1) * 256]
                                          .rearrange("(a p) f -> p a f", p=128), f"wl{tag}")
                                    P.wait("tensor", f"wl{tag}", 16 * (nw + 1))
                                    if half == 0 and it >= 1:
                                        P.wait("tensor", f"pe{tag}", it)
                                    for ct in range(DT):
                                        for tt in range(GS):
                                            last = (ct == DT - 1 and tt == GS - 1)
                                            P.op("tensor", "matmul",
                                                 ob[tt][:, half * 256:(half + 1) * 256],
                                                 yT[:, ct * TLOC + (grp * GS + tt) * 128:
                                                    ct * TLOC + (grp * GS + tt + 1) * 128],
                                                 wr2[bw][:, ct * 256:(ct + 1) * 256],
                                                 start=(ct == 0), stop=(ct == DT - 1),
                                                 inc=((f"wh{half}{tag}", 1) if last else None))
                                P.wait("vector", f"wh1{tag}", it + 1)
                                P.wait("vector", f"dxl{tag}", 16 * (it + 1))
                                if add_bo:
                                    P.wait("vector", f"dbo{tag}", 16 * (it + 1))
                                if it >= 1:
                                    P.wait("vector", f"sd{tag}", 16 * it)
                                for tt in range(GS):
                                    sl = slice(tt * 512, (tt + 1) * 512)
                                    last = ("pe" + tag, 1) if (tt == GS - 1 and not add_bo) else None
                                    P.op("vector", "tensor_tensor", xs[:, sl], ob[tt][:],
                                         xl[:, sl], ALU.add, inc=last)
                                    if add_bo:
                                        P.drain("vector")
                                        P.op("vector", "tensor_tensor", xs[:, sl], xs[:, sl],
                                             bo_st[:], ALU.add,
                                             inc=(("pe" + tag, 1) if tt == GS - 1 else None))
                                P.wait("sync", f"pe{tag}", it + 1)
                                P.dma("sync", dst_x.rearrange("(a p) d -> p a d", p=128)
                                      [:, grp * GS:(grp + 1) * GS, fch * 512:(fch + 1) * 512],
                                      xs[:, 0:GS * 512].rearrange("p (a f) -> p a f", a=GS),
                                      f"sd{tag}")
                        P.wait("sync", f"sd{tag}", 16 * NGrp * 8)

                lv = P.dma("sync", yT[:].rearrange("p (a t) -> p a t", a=DT),
                           afa.rearrange("(a p) t -> p a t", p=128), "dafl")
                P.wait("tensor", "dafl", lv)
                out_proj(woT, True, x0, x1d, True, "A4")

                P.barrier()

                def ffn_epilogue(src_dram, dst_dram, tag, build_next_yt):
                    eytv = []
                    with contextlib.ExitStack() as phe:
                        xt = XB["xt"] = [sb(f"xt{tag}{i}", [128, D], F32, phe) for i in range(2)]
                        ybf = XB["ybf"] = [sb(f"ybf{tag}{i}", [128, D], BF16, phe)
                                           for i in range(2)]
                        awr_sb = AWB["awr"] = sb(f"awr{tag}", [128, DT * (ER + NE)], BF16, phe)
                        ball_sb = AWB["ball"] = sb(f"ball{tag}", [ER, D], BF16, phe)
                        P.dma("gpsimd", awr_sb[:].rearrange("p (a c) -> p a c", a=DT),
                              awr.rearrange("a p c -> p a c"), "dCg")
                        P.dma("gpsimd", ball_sb[:], ballT, "dCg")
                        awbv = P.cur("dCg")
                        P.wait("tensor", "dCg", awbv)
                        TP["banks"] = [ps(f"tp{tag}_{i}", [128, 1024], BF16, phe)
                                       for i in range(2)]
                        dl_ps = [ps(f"dl{tag}_{i}", [128, 512], F32, phe) for i in range(2)]
                        sm_ps = ps(f"sm{tag}", [128, 512], F32, phe)
                        mid_ps = sm_ps[0:ER + NE, 0:128]
                        lgt_ps = sm_ps[:, 128:132]
                        gt_ps = sm_ps[0:NE, 132:260]
                        grep_ps = sm_ps[0:ER, 260:388]
                        msb = sb(f"msb{tag}", [ER + NE, 128], F32, phe)
                        lgt_sb = sb(f"lgt{tag}", [128, NE], F32, phe)
                        gsc = sb(f"gsc{tag}", [128, 8], F32, phe)
                        pe_sb = sb(f"pes{tag}", [128, NE], F32, phe)
                        gmask = sb(f"gmk{tag}", [128, NE], F32, phe)
                        g_sb = sb(f"gsb{tag}", [128, NE], F32, phe)
                        gt_sb = sb(f"gts{tag}", [NE, 128], BF16, phe)
                        gmid_sb = sb(f"gmd{tag}", [ER, 128], BF16, phe)

                        for k in range(nt):
                            b = k % 2
                            if k >= 2:
                                P.wait("sync", f"dle{tag}", 8 * (k - 1))
                                P.wait("sync", f"xst{tag}", 16 * (k - 1))
                                if build_next_yt:
                                    P.wait("sync", f"nr2{tag}", k - 1)
                            lv = P.dma("sync", xt[b][:], src_dram[k * 128:(k + 1) * 128, :],
                                       f"dx{tag}")
                            P.wait("vector", f"dx{tag}", lv)
                            if k >= 2:
                                P.wait("vector", f"nrm{tag}", k - 1)
                            ln_tile(xt[b], b)
                            P.wait("scalar", "lnv", P.cur("lnv"))
                            if k >= 2:
                                P.wait("scalar", "tpe", eytv[2 * (k - 2) + 1])
                            P.op("scalar", "activation", ybf[b][:], xt[b][:], AF.Identity,
                                 bias=nmr[b][:, 0:1], scale=rstd[b][:, 0:1],
                                 inc=(f"nrm{tag}", 1))
                            P.wait("tensor", f"nrm{tag}", k + 1)
                            ytv = transpose_tile_to_yt(ybf[b], k)
                            eytv.append(ytv)
                            P.wait("tensor", "tpe", ytv)
                            if k > 0:
                                P.wait("tensor", f"mde{tag}", k)
                            for dt_i in range(DT):
                                P.op("tensor", "matmul", mid_ps,
                                     awr_sb[:, dt_i * (ER + NE):(dt_i + 1) * (ER + NE)],
                                     yT[:, dt_i * TLOC + k * 128:dt_i * TLOC + (k + 1) * 128],
                                     start=(dt_i == 0), stop=(dt_i == DT - 1),
                                     inc=([(f"mdf{tag}", 1)] if dt_i == DT - 1 else None))
                            P.wait("vector", f"mdf{tag}", k + 1)
                            if k > 0:
                                P.wait("vector", f"lgf{tag}", k)
                            P.op("vector", "tensor_copy", msb[:], mid_ps, inc=(f"mde{tag}", 1))
                            P.wait("tensor", f"mde{tag}", k + 1)
                            P.op("tensor", "transpose", lgt_ps, msb[ER:ER + NE, :],
                                 id32_sb[ER:ER + NE, ER:ER + NE], inc=(f"lgf{tag}", 1))
                            P.wait("scalar", f"lgf{tag}", k + 1)
                            P.op("scalar", "copy", lgt_sb[:], lgt_ps, inc=(f"lge{tag}", 1))
                            P.wait("vector", f"lge{tag}", k + 1)
                            l0, l1 = lgt_sb[:, 0:1], lgt_sb[:, 1:2]
                            l2, l3 = lgt_sb[:, 2:3], lgt_sb[:, 3:4]
                            P.op("vector", "tensor_tensor", gsc[:, 0:1], l0, l1, ALU.max)
                            P.op("vector", "tensor_tensor", gsc[:, 1:2], l0, l1, ALU.min)
                            P.op("vector", "tensor_tensor", gsc[:, 2:3], l2, l3, ALU.max)
                            P.op("vector", "tensor_tensor", gsc[:, 3:4], l2, l3, ALU.min)
                            P.drain("vector")
                            P.op("vector", "tensor_tensor", gsc[:, 4:5], gsc[:, 0:1],
                                 gsc[:, 2:3], ALU.max)
                            P.op("vector", "tensor_tensor", gsc[:, 5:6], gsc[:, 0:1],
                                 gsc[:, 2:3], ALU.min)
                            P.op("vector", "tensor_tensor", gsc[:, 6:7], gsc[:, 1:2],
                                 gsc[:, 3:4], ALU.max)
                            P.drain("vector")
                            P.op("vector", "tensor_tensor", gsc[:, 5:6], gsc[:, 5:6],
                                 gsc[:, 6:7], ALU.max, inc=(f"gmx{tag}", 1))
                            P.wait("scalar", f"gmx{tag}", k + 1)
                            P.op("scalar", "mul", gsc[:, 7:8], gsc[:, 4:5], -1.0)
                            P.drain("scalar")
                            P.op("scalar", "activation", pe_sb[:], lgt_sb[:], AF.Exp,
                                 bias=gsc[:, 7:8], inc=(f"pex{tag}", 1))
                            P.wait("vector", f"pex{tag}", k + 1)
                            P.op("vector", "tensor_scalar", gmask[:], lgt_sb[:], gsc[:, 5:6],
                                 None, ALU.is_ge)
                            P.drain("vector")
                            P.op("vector", "tensor_tensor", g_sb[:], pe_sb[:], gmask[:],
                                 ALU.mult)
                            P.drain("vector")
                            P.op("vector", "reduce_sum", gsc[:, 6:7], g_sb[:], axis=AX)
                            P.drain("vector")
                            P.op("vector", "reciprocal", gsc[:, 7:8], gsc[:, 6:7])
                            P.drain("vector")
                            P.op("vector", "tensor_scalar", g_sb[:], g_sb[:], gsc[:, 7:8],
                                 None, ALU.mult, inc=(f"gv{tag}", 1))
                            P.wait("tensor", f"gv{tag}", k + 1)
                            P.op("tensor", "transpose", gt_ps, g_sb[:], id32_sb[:],
                                 inc=(f"gtf{tag}", 1))
                            P.wait("scalar", f"gtf{tag}", k + 1)
                            P.op("scalar", "copy", gt_sb[:], gt_ps, inc=(f"gte{tag}", 1))
                            P.wait("tensor", f"gte{tag}", k + 1)
                            P.op("tensor", "matmul", grep_ps, e4_sb[:], gt_sb[:],
                                 start=True, stop=True, inc=(f"grf{tag}", 1))
                            P.wait("vector", f"grf{tag}", k + 1)
                            P.op("vector", "tensor_tensor", gmid_sb[:], grep_ps, msb[0:ER, :],
                                 ALU.mult, inc=(f"gmv{tag}", 1))
                            P.wait("tensor", f"gmv{tag}", k + 1)
                            for fch in range(8):
                                bb = fch % 2
                                ndl = 8 * k + fch
                                if ndl >= 2:
                                    P.wait("tensor", f"dle{tag}", ndl - 1)
                                P.op("tensor", "matmul", dl_ps[bb][:], gmid_sb[:],
                                     ball_sb[:, fch * 512:(fch + 1) * 512],
                                     start=True, stop=True, inc=(f"dlf{tag}", 1))
                                P.wait("vector", f"dlf{tag}", P.cur(f"dlf{tag}"))
                                P.op("vector", "scalar_tensor_tensor",
                                     xt[b][:, fch * 512:(fch + 1) * 512], dl_ps[bb][:],
                                     SCALE_FFN, xt[b][:, fch * 512:(fch + 1) * 512],
                                     ALU.mult, ALU.add, inc=(f"dle{tag}", 1))
                            P.wait("sync", f"dle{tag}", 8 * (k + 1))
                            P.dma("sync", dst_dram[k * 128:(k + 1) * 128, :], xt[b][:],
                                  f"xst{tag}")
                            if build_next_yt:
                                P.drain("vector")
                                ln_tile(xt[b], b)
                                P.wait("scalar", "lnv", P.cur("lnv"))
                                P.wait("scalar", "tpe", ytv)
                                P.op("scalar", "activation", ybf[b][:], xt[b][:], AF.Identity,
                                     bias=nmr[b][:, 0:1], scale=rstd[b][:, 0:1],
                                     inc=(f"nr2{tag}", 1))
                                P.wait("tensor", f"nr2{tag}", k + 1)
                                eytv.append(transpose_tile_to_yt(ybf[b], k))
                            else:
                                eytv.append(ytv)
                        P.wait("sync", f"xst{tag}", 16 * nt)

                ffn_epilogue(x1d, x2d, "E1", True)
                yt3_done = ("tpe", P.cur("tpe"))
                P.barrier()

                with contextlib.ExitStack() as phb:
                    WB["b"] = [sb(f"wblkB{i}", [128, DT * 128], BF16, phb) for i in range(2)]
                    PJ["banks"] = [ps(f"pjB_{i}", [128, 512], F32, phb) for i in range(TC)]
                    TP["banks"] = [ps(f"tpB_{i}", [128, 1024], BF16, phb) for i in range(2)]
                    cos_sb = sb("cos_sb", [64, TLOC], F32, phb)
                    sin_sb = sb("sin_sb", [64, TLOC], F32, phb)
                    qt_sb = sb("qt_sb", [128, TLOC], BF16, phb)
                    vft2 = sb("vft2", [128, TLOC], BF16, phb)
                    vtok2 = sb("vtok2", [128, TLOC], BF16, phb)
                    r1 = sb("r1", [64, 512], F32, phb)
                    r2 = sb("r2", [64, 512], F32, phb)
                    P.dma("sync", cos_sb[:], cosf, "dC")
                    P.dma("sync", sin_sb[:], sinf, "dC")
                    P.wait("vector", "dC", P.cur("dC"))

                    def rope_evac(dst_bf):
                        prj = PJ["banks"]
                        base = P.cur("pjf") - TC
                        for tc in range(TC):
                            sl = slice(tc * 512, (tc + 1) * 512)
                            cc_ = cos_sb[:, sl]
                            ss_ = sin_sb[:, sl]
                            P.wait("vector", "pjf", base + tc + 1)
                            P.drain("vector")
                            P.op("vector", "tensor_tensor", r1[:], prj[tc][0:64, :], cc_,
                                 ALU.mult)
                            P.op("vector", "tensor_tensor", r2[:], prj[tc][64:128, :], ss_,
                                 ALU.mult)
                            P.drain("vector")
                            P.op("vector", "tensor_tensor", dst_bf[0:64, sl], r1[:], r2[:],
                                 ALU.subtract)
                            P.drain("vector")
                            P.op("vector", "tensor_tensor", r1[:], prj[tc][0:64, :], ss_,
                                 ALU.mult)
                            P.op("vector", "tensor_tensor", r2[:], prj[tc][64:128, :], cc_,
                                 ALU.mult)
                            P.drain("vector")
                            P.op("vector", "tensor_tensor", dst_bf[64:128, sl], r1[:], r2[:],
                                 ALU.add, inc=("pje", 1))

                    P.wait("tensor", yt3_done[0], yt3_done[1])
                    nspill = 0
                    for g in range(KV):
                        if nspill > 0:
                            P.wait("vector", "spl", 16 * nspill)
                        proj_block(wqkv, (DT + g) * 128, TC)
                        rope_evac(qt_sb)
                        P.wait("sync", "pje", P.cur("pje"))
                        P.dma("sync", kx[g * 128:(g + 1) * 128, :], qt_sb[:], "spl")
                        nspill += 1
                    vtv2 = 0
                    for g in range(KV):
                        if g > 0:
                            P.wait("vector", "tpe", vtv2)
                        proj_block(wqkv, (DT + KV + g) * 128, TC)
                        plain = PJ["banks"]
                        vbase = P.cur("pjf") - TC
                        for tc in range(TC):
                            P.wait("vector", "pjf", vbase + tc + 1)
                            P.op("vector", "tensor_copy", vft2[:, tc * 512:(tc + 1) * 512],
                                 plain[tc][:], inc=("pje", 1))
                        P.wait("tensor", "pje", P.cur("pje"))
                        if g > 0:
                            P.wait("tensor", "spl", 16 * nspill)
                        vtv2 = tok_transpose(vft2, vtok2)
                        P.wait("sync", "tpe", P.cur("tpe"))
                        P.dma("sync",
                              vx.rearrange("(a p) f -> p a f", p=128)[:, :, g * 128:(g + 1) * 128],
                              vtok2[:].rearrange("p (a f) -> p a f", a=nt), "spl")
                        nspill += 1
                    P.wait("gpsimd", "spl", 16 * nspill)
                    P.sem("dsh")
                    wvK = (P.inc("dsh", 16), None)
                    wvV = (P.inc("dsh", 16), P.inc("dsh", 16))

                    def stageB(g, wvK=wvK, wvV=wvV):
                        rr = g.alloc_register()
                        dsh = P.sem_handles["dsh"]
                        g.reg_load(rr, rks_sb[0:1, 0:1])
                        with g.If_eq(rr, 0):
                            g.dma_start(out=kxg[0:KV * HD, :], in_=kx).then_inc(dsh, 16)
                            g.dma_start(out=vxg[0:TLOC, :], in_=vx).then_inc(dsh, 16)
                        with g.Else():
                            g.dma_start(out=kxg[KV * HD:2 * KV * HD, :],
                                        in_=kx).then_inc(dsh, 16)
                            g.dma_start(out=vxg[TLOC:2 * TLOC, :],
                                        in_=vx).then_inc(dsh, 16)
                        g.wait_ge(dsh, wvV[0])
                        with g.If_eq(rr, 0):
                            g.dma_start(out=sh_fl[2:3, :],
                                        in_=toks_sb[0:1, :]).then_inc(dsh, 16)
                        with g.Else():
                            g.dma_start(out=sh_fl[3:4, :],
                                        in_=toks_sb[0:1, :]).then_inc(dsh, 16)
                        g.wait_ge(dsh, wvV[1])
                    P.raw("gpsimd", stageB)
                    P.wait("sync", "spl", 16 * nspill)

                P.barrier()

                with contextlib.ExitStack() as phc:
                    s_ps4 = [ps(f"s4_{i}", [128, 512], F32, phc) for i in range(3)]
                    fq_ps = ps("fq_ps", [128, 512], F32, phc)
                    pt_ps2 = [ps(f"pt2_{i}", [128, 1024], BF16, phc) for i in range(2)]
                    ov_ps2 = [ps(f"ov_ps{i}", [128, 512], F32, phc) for i in range(2)]
                    mask_sb = sb("mask_sb", [128, 1024], BF16, phc)
                    kg_sb = sb("kg_sb", [128, 2 * TLOC], BF16, phc)
                    vg_sb = sb("vg_sb", [128, 2 * TLOC], BF16, phc)
                    S_sb = sb("S_sb", [128, 4096], F32, phc)
                    P2 = [sb(f"P2_{i}", [128, 4096], BF16, phc) for i in range(2)]
                    PT_sb = sb("PT_sb", [128, 256 * nt], BF16, phc)
                    qh_sb = [sb(f"qh_{i}", [128, 512], BF16, phc) for i in range(2)]
                    zb2 = [sb(f"zb{i}", [128, NE], F32, phc) for i in range(2)]
                    izc = sb("izc", [128, 1], F32, phc)
                    mrow = sb("mrow", [128, 1], F32, phc)
                    nmsc = sb("nmsc", [128, 1], F32, phc)
                    wfq = sb("wfq", [128, DT * 128], BF16, phc)
                    cs_c = sb("cs_c", [64, 512], BF16, phc)
                    cs_s = sb("cs_s", [64, 512], BF16, phc)
                    qs1 = sb("qs1", [128, 512], BF16, phc)
                    fr1_t = sb("fr1_t", [64, 512], F32, phc)
                    fr2_t = sb("fr2_t", [64, 512], F32, phc)
                    fr1 = fr1_t[:]
                    fr2 = fr2_t[:]
                    afs2 = [sb(f"afs1_{i}", [128, 512], BF16, phc) for i in range(2)]
                    P.dma("gpsimd", mask_sb[:, 0:512], maskd[0], "dCg")
                    P.dma("gpsimd", mask_sb[:, 512:1024], maskd[1], "dCg")
                    P.wait("vector", "dCg", P.cur("dCg"))
                    P.wait("tensor", yt3_done[0], yt3_done[1])
                    P.raw("sync", mk_poll(2))

                    fill_state = {"h": 0, "tc": 0, "nqs": 0}
                    qspl = {}

                    def filler_step():
                        h = fill_state["h"]
                        if h >= HEADS:
                            return
                        tc = fill_state["tc"]
                        if tc == 0:
                            if h >= 1:
                                P.wait("gpsimd", "fqf", TC * h)
                            P.dma("gpsimd",
                                  wfq[:].rearrange("p (a f) -> p a f", a=DT),
                                  wqkv[:, h * 128:(h + 1) * 128]
                                  .rearrange("(a p) f -> p a f", p=128), "fqw")
                        nqs0 = fill_state["nqs"]
                        P.wait("gpsimd", "fqe", nqs0)
                        P.dma("gpsimd", cs_c[:], cosf[:, tc * 512:(tc + 1) * 512], "fcs")
                        P.dma("gpsimd", cs_s[:], sinf[:, tc * 512:(tc + 1) * 512], "fcs")
                        P.wait("tensor", "fqw", 16 * (h + 1))
                        P.wait("tensor", "fqe", TC * h + tc)
                        for dt_i in range(DT):
                            P.op("tensor", "matmul", fq_ps[:],
                                 wfq[:, dt_i * 128:(dt_i + 1) * 128],
                                 yT[:, dt_i * TLOC + tc * 512:dt_i * TLOC + (tc + 1) * 512],
                                 start=(dt_i == 0), stop=(dt_i == DT - 1),
                                 inc=(("fqf", 1) if dt_i == DT - 1 else None))
                        nqs = fill_state["nqs"]
                        sl = slice(tc * 512, (tc + 1) * 512)
                        P.wait("vector", "fqf", TC * h + tc + 1)
                        P.wait("vector", "fcs", 32 * (nqs + 1))
                        if nqs >= 1:
                            P.wait("vector", "qsd", 16 * nqs)
                        P.drain("vector")
                        P.op("vector", "tensor_tensor", fr1, fq_ps[0:64, :], cs_c[:],
                             ALU.mult)
                        P.op("vector", "tensor_tensor", fr2, fq_ps[64:128, :], cs_s[:],
                             ALU.mult)
                        P.drain("vector")
                        P.op("vector", "tensor_tensor", qs1[0:64, :], fr1, fr2,
                             ALU.subtract)
                        P.drain("vector")
                        P.op("vector", "tensor_tensor", fr1, fq_ps[0:64, :], cs_s[:],
                             ALU.mult)
                        P.op("vector", "tensor_tensor", fr2, fq_ps[64:128, :], cs_c[:],
                             ALU.mult)
                        P.drain("vector")
                        P.op("vector", "tensor_tensor", qs1[64:128, :], fr1, fr2,
                             ALU.add, inc=("fqe", 1))
                        P.wait("sync", "fqe", TC * h + tc + 1)
                        P.dma("sync", qTb[h * 128:(h + 1) * 128, tc * 512:(tc + 1) * 512],
                              qs1[:], "qsd")
                        fill_state["nqs"] = nqs + 1
                        if tc == TC - 1:
                            qspl[h] = P.cur("qsd")
                            fill_state["h"] = h + 1
                            fill_state["tc"] = 0
                        else:
                            fill_state["tc"] = tc + 1

                    for _ in range(4 * TC):
                        filler_step()

                    nhead = 0
                    qfree = []
                    units = [(gh, g2, hh) for gh in range(KV)
                             for g2 in range(NTG) for hh in range(GQ)]
                    qh_v = {}

                    def issue_qh(i):
                        gh_i, g2_i, hh_i = units[i]
                        h_i = gh_i * GQ + hh_i
                        if i >= 2:
                            for bi in range(3):
                                P.wait("sync", f"se{bi}", qfree[i - 2][bi])
                        P.wait("sync", "qsd", qspl[h_i])
                        P.dma("sync", qh_sb[i % 2][:],
                              qTb[h_i * 128:(h_i + 1) * 128,
                                  g2_i * 512:(g2_i + 1) * 512], "dqh")
                        qh_v[i] = P.cur("dqh")

                    issue_qh(0)
                    for ui, (gh, g2, hh) in enumerate(units):
                        if g2 == 0 and hh == 0:
                            P.wait("sync", "pvp", 4 * nhead)
                            for r in range(2):
                                P.dma("sync",
                                      kg_sb[:].rearrange("p (a f) -> p a f", f=256)
                                      [:, 0:nt, r * 128:(r + 1) * 128],
                                      kxg[r * KV * HD + gh * HD:
                                          r * KV * HD + (gh + 1) * HD,
                                          0:nt * 128]
                                      .rearrange("p (a f) -> p a f", f=128),
                                      "dkg")
                                P.dma("sync",
                                      vg_sb[:].rearrange("p (a f) -> p a f", f=256)
                                      [:, 0:nt, r * 128:(r + 1) * 128],
                                      vxg.rearrange("(a p) f -> p a f", p=128)
                                      [:, r * nt:(r + 1) * nt, gh * 128:(gh + 1) * 128],
                                      "dvg")
                            kgv = P.cur("dkg")
                            vgv = P.cur("dvg")
                        if True:
                            if True:
                                h = gh * GQ + hh
                                filler_step()
                                if ui + 1 < len(units):
                                    issue_qh(ui + 1)
                                bq_ = nhead % 2
                                P.wait("tensor", "dqh", qh_v[ui])
                                P.wait("tensor", "dkg", kgv)
                                P.wait("tensor", "dvg", vgv)
                                pnm_v = [0] * 4

                                def scores_part(kk):
                                    kloc = 4 * g2 + kk
                                    nck = kloc // 2 + 1
                                    pb = kk % 2
                                    P.wait("vector", "pex", P.cur("pex"))
                                    for c in range(nck):
                                        nsc = P.cur("sf")
                                        bs = nsc % 3
                                        if nsc >= 3:
                                            P.wait("tensor", f"se{bs}", nsc // 3)
                                        P.op("tensor", "matmul", s_ps4[bs][:],
                                             qh_sb[bq_][:, kk * 128:(kk + 1) * 128],
                                             kg_sb[:, c * 512:(c + 1) * 512],
                                             start=True, stop=True, inc=("sf", 1))
                                        if c == nck - 1:
                                            P.wait("vector", "sf", P.cur("sf"))
                                            P.op("vector", "tensor_tensor",
                                                 S_sb[:, c * 512:(c + 1) * 512], s_ps4[bs][:],
                                                 mask_sb[:, (kloc % 2) * 512:
                                                         (kloc % 2 + 1) * 512],
                                                 ALU.add, inc=(f"se{bs}", 1))
                                        else:
                                            P.wait("scalar", "sf", P.cur("sf"))
                                            P.op("scalar", "copy",
                                                 S_sb[:, c * 512:(c + 1) * 512], s_ps4[bs][:],
                                                 inc=(f"se{bs}", 1))
                                    sev = [P.cur(f"se{bi}") for bi in range(3)]
                                    for bi in range(3):
                                        P.wait("vector", f"se{bi}", sev[bi])
                                    P.drain("vector")
                                    P.op("vector", "reduce_max", mrow[:], S_sb[:, 0:nck * 512],
                                         axis=AX)
                                    P.drain("vector")
                                    P.op("vector", "tensor_scalar", nmsc[:], mrow[:],
                                         -SM_SCALE, None, ALU.mult, inc=("smv", 1))
                                    P.wait("scalar", "smv", P.cur("smv"))
                                    P.wait("scalar", "pte", P.cur("ptf"))
                                    if kk == 0 and nhead >= 2:
                                        P.wait("scalar", "izv", 4 * (nhead - 1))
                                    P.op("scalar", "activation", P2[pb][:, 0:nck * 512],
                                         S_sb[:, 0:nck * 512], AF.Exp, bias=nmsc[:, 0:1],
                                         scale=SM_SCALE,
                                         accum_out=zb2[nhead % 2][:, kk:kk + 1],
                                         inc=("pex", 1))
                                    P.wait("vector", "pex", P.cur("pex"))
                                    P.drain("vector")
                                    P.op("vector", "reciprocal", izc[:],
                                         zb2[nhead % 2][:, kk:kk + 1], inc=("izv", 1))
                                    P.drain("vector")
                                    P.op("vector", "tensor_scalar", P2[pb][:, 0:nck * 512],
                                         P2[pb][:, 0:nck * 512], izc[:, 0:1], None, ALU.mult,
                                         inc=("pnm", 1))
                                    pnm_v[kk] = P.cur("pnm")

                                def post_part(kk):
                                    kloc = 4 * g2 + kk
                                    nck = kloc // 2 + 1
                                    pb = kk % 2
                                    P.wait("tensor", "pnm", pnm_v[kk])
                                    P.wait("tensor", "pvp", 4 * nhead + kk)
                                    nT = nck * 4
                                    for pk in range((nT + 7) // 8):
                                        n8 = min(8, nT - pk * 8)
                                        srcs = [P2[pb][:, (pk * 8 + j) * 128:
                                                       (pk * 8 + j + 1) * 128]
                                                for j in range(n8)]
                                        bank = P.cur("ptf") % 2
                                        if P.cur("ptf") >= 2:
                                            P.wait("tensor", "pte", P.cur("ptf") - 1)
                                        tp = pt_ps2[bank]
                                        for j in range(n8):
                                            P.op("tensor", "transpose",
                                                 tp[:, j * 128:(j + 1) * 128], srcs[j],
                                                 ident_sb[:])
                                        meth, args, kw, incs = P.ops["tensor"][-1]
                                        incs.append((P.sem("ptf"), 1))
                                        P.counts["ptf"] += 1
                                        P.wait("vector", "ptf", P.cur("ptf"))
                                        P.op("vector", "tensor_copy",
                                             PT_sb[:, pk * 8 * 128:(pk * 8 + n8) * 128],
                                             tp[:].rearrange("p (a f) -> p a f", f=128)
                                             [:, 0:n8, :], inc=("pte", 1))
                                    P.wait("tensor", "pte", P.cur("pte"))
                                    if kk == 0 and nhead >= 2:
                                        P.wait("tensor", "ove", nhead - 1)
                                    for c in range(nck):
                                        for tt in range(4):
                                            T = 4 * c + tt
                                            last = (c == nck - 1 and tt == 3)
                                            P.op("tensor", "matmul",
                                                 ov_ps2[nhead % 2][:, kk * 128:
                                                                   (kk + 1) * 128],
                                                 vg_sb[:, T * 128:(T + 1) * 128],
                                                 PT_sb[:, T * 128:(T + 1) * 128],
                                                 start=(c == 0 and tt == 0),
                                                 stop=last,
                                                 inc=(("pvp", 1) if last else None))

                                for step in range(5):
                                    if step < 4:
                                        scores_part(step)
                                    if step >= 1:
                                        post_part(step - 1)
                                hb = nhead % 2
                                P.wait("vector", "pvp", P.cur("pvp"))
                                if nhead >= 2:
                                    P.wait("vector", "dab", 16 * (nhead - 1))
                                P.op("vector", "tensor_copy", afs2[nhead % 2][:],
                                     ov_ps2[hb][:], inc=("ove", 1))
                                P.wait("sync", "ove", P.cur("ove"))
                                P.dma("sync", afb[h * 128:(h + 1) * 128,
                                               g2 * 512:(g2 + 1) * 512],
                                      afs2[nhead % 2][:], "dab")
                                qfree.append(tuple(P.cur(f"se{bi}") for bi in range(3)))
                                nhead += 1
                    P.wait("sync", "dab", P.cur("dab"))

                P.barrier()

                lv2 = P.dma("sync", yT[:].rearrange("p (a t) -> p a t", a=DT),
                            afb.rearrange("(a p) t -> p a t", p=128), "dafl")
                P.wait("tensor", "dafl", lv2)
                out_proj(wo2, True, x2d, x3d, False, "B4")
                P.barrier()

                ffn_epilogue(x3d, out, "E2", False)

            P.finalize(block)

    return nc



def _prep_inputs(inputs, nt=16):
    TLOC = nt * 128
    hs = inputs["L0_attn_a_hidden_states"]
    B = hs.shape[0]
    f32 = np.float32
    Wq = inputs["L0_attn_a_Wq"]; bq = inputs["L0_attn_a_bq"]
    Wk = inputs["L0_attn_a_Wk"]; bk = inputs["L0_attn_a_bk"]
    Wv = inputs["L0_attn_a_Wv"]; bv = inputs["L0_attn_a_bv"]
    Wo = inputs["L0_attn_a_Wo"]; bo = inputs["L0_attn_a_bo"]
    wqkv = inputs["L0_attn_b_w_qkv"]; w_o = inputs["L0_attn_b_w_o"]
    A = inputs["L0_ffn_A"]; Bm = inputs["L0_ffn_B"]; Wr = inputs["L0_ffn_Wr"]

    wqT = np.ascontiguousarray(np.asarray(Wq).T, dtype=f32)
    wkT = np.ascontiguousarray(np.asarray(Wk).T, dtype=f32)
    wvT = np.ascontiguousarray(np.asarray(Wv).T, dtype=f32)
    woT = np.ascontiguousarray(np.asarray(Wo).T, dtype=f32)
    wo2 = np.ascontiguousarray(np.asarray(w_o), dtype=f32)
    perm = np.concatenate([np.arange(0, HD, 2), np.arange(1, HD, 2)])
    wqkv_p = np.array(wqkv, dtype=f32, copy=True)
    for h in range(HEADS):
        wqkv_p[:, h * HD:(h + 1) * HD] = wqkv_p[:, h * HD + perm]
    for g in range(KV):
        base = D + g * HD
        wqkv_p[:, base:base + HD] = wqkv_p[:, base + perm]
    wqkv_p = np.ascontiguousarray(wqkv_p)

    A = np.asarray(A, dtype=f32); Bm = np.asarray(Bm, dtype=f32); Wr = np.asarray(Wr, dtype=f32)
    AWr = np.concatenate([A.reshape(ER, D).T, Wr.T], axis=1).astype(f32)
    awr = np.ascontiguousarray(AWr.reshape(DT, 128, ER + NE))
    ballT = np.ascontiguousarray(Bm.transpose(0, 2, 1).reshape(ER, D), dtype=f32)
    bqT = np.ascontiguousarray(np.asarray(bq).reshape(DT, 128).T, dtype=f32)
    bkT = np.ascontiguousarray(np.asarray(bk).reshape(KV, 128).T, dtype=f32)
    bvT = np.ascontiguousarray(np.asarray(bv).reshape(KV, 128).T, dtype=f32)
    bo_bc = np.ascontiguousarray(np.broadcast_to(np.asarray(bo, dtype=f32)[None, :], (128, D)))
    identb = np.eye(128, dtype=ml_dtypes.bfloat16)
    ident32 = np.eye(128, dtype=f32)
    e4m = np.zeros((NE, ER), dtype=ml_dtypes.bfloat16)
    for e in range(NE):
        e4m[e, e * LR:(e + 1) * LR] = 1
    onesb = np.ones((128, 1), dtype=ml_dtypes.bfloat16)
    import math as _m
    fconst = np.ascontiguousarray(np.broadcast_to(np.array([1e-5, -0.5, -1.0, 1.0 / _m.sqrt(HD), 1.0, 0.0, SCALE_FFN, 1e-10], dtype=f32)[None, :], (128, 8)))

    shared = dict(wqT=wqT, wkT=wkT, wvT=wvT, woT=woT, wqkv=wqkv_p, wo2=wo2,
                  awr=awr, ballT=ballT, bqT=bqT, bkT=bkT, bvT=bvT, bo_bc=bo_bc,
                  identb=identb, ident32=ident32, e4=e4m, onesb=onesb,
                  fconst=fconst)

    inv_freq = 10000.0 ** (-(np.arange(0, HD, 2, dtype=np.float64) / HD))
    token_arr = np.full((1, 8), np.random.randint(1, 2**31 - 2, dtype=np.int64),
                        np.int32)
    in_maps, perms = [], []
    for c in range(2 * B):
        b, r = c // 2, c % 2
        tiles = np.arange(nt) * 2 + r
        tok = (tiles[:, None] * 128 + np.arange(128)[None, :]).reshape(-1)
        perms.append((b, tok))
        x0 = np.ascontiguousarray(np.asarray(hs[b], dtype=f32)[tok, :])
        ang = tok[None, :].astype(np.float64) * inv_freq[:, None]
        im = dict(shared)
        maskm = np.zeros((2, 128, 512), dtype=f32)
        col = np.arange(512)[None, :]
        row = np.arange(128)[:, None]
        for m in range(2):
            q = 2 * m + r
            maskm[m] = np.where(col <= 128 * q + row, 0.0, NEG).astype(f32)
        im.update(x0=x0, cosf=np.ascontiguousarray(np.cos(ang).astype(f32)),
                  sinf=np.ascontiguousarray(np.sin(ang).astype(f32)), maskd=maskm,
                  rk=np.full((1, 8), r, np.int32), tokin=token_arr)
        in_maps.append(im)
    return in_maps, perms


def kernel(**inputs):
    nt = 16
    in_maps, perms = _prep_inputs(inputs, nt=nt)
    nc = build(nt=nt)
    res = run_bass_kernel_spmd(nc, in_maps, core_ids=list(range(8)))
    hs = inputs["L0_attn_a_hidden_states"]
    B, SEQ, _ = hs.shape
    outp = np.empty((B, SEQ, D), dtype=np.float32)
    for c, (b, tok) in enumerate(perms):
        outp[b, tok, :] = res.results[c]["out"]
    return outp


# version 18
# speedup vs baseline: 1.6863x; 1.6863x over previous
import contextlib
import math
import numpy as np
import ml_dtypes

import concourse.bass as bass
import concourse.mybir as mybir
from concourse.bass_utils import run_bass_kernel_spmd

F32 = mybir.dt.float32
BF16 = mybir.dt.bfloat16
AF = mybir.ActivationFunctionType
ALU = mybir.AluOpType
AX = mybir.AxisListType.X

D = 4096
HEADS = 32
KV = 8
HD = 128
GQ = HEADS // KV
NE = 4
LR = 16
ER = NE * LR
SCALE_FFN = 4.0
SM_SCALE = 1.0 / math.sqrt(HD)
NEG = -1.0e30
EPS_LN = 1e-5
EPS_LA = 1e-10
DT = D // 128

ENGINES = ("tensor", "vector", "scalar", "gpsimd", "sync")


class Prog:

    def __init__(self, nc, stack):
        self.nc = nc
        self.ops = {e: [] for e in ENGINES}
        self.sem_handles = {}
        self.counts = {}
        self._stack = stack

    def sem(self, name):
        if name not in self.sem_handles:
            self.sem_handles[name] = self._stack.enter_context(self.nc.semaphore(name))
            self.counts[name] = 0
        return self.sem_handles[name]

    def inc(self, name, n=1):
        self.sem(name)
        self.counts[name] += n
        return self.counts[name]

    def cur(self, name):
        self.sem(name)
        return self.counts[name]

    def op(self, eng, meth, *args, inc=None, **kw):
        incs = []
        outc = []
        if inc:
            if isinstance(inc, tuple):
                inc = [inc]
            for (s, n) in inc:
                c = self.inc(s, n)
                incs.append((self.sem_handles[s], n))
                outc.append(c)
        self.ops[eng].append((meth, args, kw, incs[:1]))
        for extra in incs[1:]:
            self.ops[eng].append(("nop", (), {}, [extra]))
        return outc[0] if len(outc) == 1 else outc

    def wait(self, eng, sem_name, value):
        if value <= 0:
            return
        h = self.sem(sem_name)
        self.ops[eng].append(("wait_ge", (h, value), {}, []))

    def drain(self, eng):
        self.ops[eng].append(("drain", (), {}, []))

    def raw(self, eng, fn):
        self.ops[eng].append(("__raw__", (fn,), {}, []))

    def dma(self, eng, out, in_, sem_name, **kw):
        prev = self.cur(sem_name)
        if prev > 0:
            self.wait(eng, sem_name, prev)
        return self.op(eng, "dma_start", inc=(sem_name, 16), out=out, in_=in_, **kw)

    def barrier(self):
        self.sem("bar")
        for e in ENGINES:
            self.drain(e)
            self.op(e, "nop", inc=("bar", 1))
        tgt = self.cur("bar")
        for e in ENGINES:
            self.wait(e, "bar", tgt)

    def finalize(self, block):
        for e in ENGINES:
            ops = self.ops[e]

            def make(ops=ops):
                def run(eng):
                    for (meth, args, kw, incs) in ops:
                        if meth == "__raw__":
                            args[0](eng)
                            continue
                        inst = getattr(eng, meth)(*args, **kw)
                        for (h, n) in incs:
                            inst.then_inc(h, n)
                return run
            getattr(block, e)(make())


def build(nt=16, debug=None, n_pairs=4):
    debug = debug or {}
    TLOC = nt * 128
    TC = TLOC // 512
    NTG = nt // 4
    NG = TLOC // 1024
    nc = bass.Bass("TRN2", target_bir_lowering=False)

    def din(name, shape, dt=F32):
        return nc.dram_tensor(name, shape, dt, kind="ExternalInput").ap()

    def dint(name, shape, dt=F32):
        return nc.dram_tensor(name, shape, dt, kind="Internal").ap()

    def dout(name, shape, dt=F32):
        return nc.dram_tensor(name, shape, dt, kind="ExternalOutput").ap()

    x0 = din("x0", [TLOC, D])
    wqT = din("wqT", [D, D])
    wkT = din("wkT", [D, KV * HD])
    wvT = din("wvT", [D, KV * HD])
    woT = din("woT", [D, D])
    wqkv = din("wqkv", [D, D + 2 * KV * HD])
    wo2 = din("wo2", [D, D])
    awr = din("awr", [DT, 128, ER + NE])
    ballT = din("ballT", [ER, D])
    bqT = din("bqT", [128, DT])
    bkT = din("bkT", [128, KV])
    bvT = din("bvT", [128, KV])
    bo_bc = din("bo_bc", [128, D])
    identb = din("identb", [128, 128], BF16)
    ident32 = din("ident32", [128, 128], F32)
    cosf = din("cosf", [64, TLOC])
    sinf = din("sinf", [64, TLOC])
    maskd = din("maskd", [2, 128, 512])
    e4 = din("e4", [NE, ER], BF16)
    onesb = din("onesb", [128, 1], BF16)
    fconst = din("fconst", [128, 8])
    FCONST_VALS = [EPS_LN, -0.5, -1.0, SM_SCALE, 1.0, 0.0, SCALE_FFN, EPS_LA]

    x1d = dint("x1d", [TLOC, D])
    x2d = dint("x2d", [TLOC, D])
    x3d = dint("x3d", [TLOC, D])
    qTb = dint("qTb", [D, TLOC], BF16)
    afa = dint("afa", [D, TLOC], BF16)
    afb = dint("afb", [D, TLOC], BF16)
    kx = dint("kx", [KV * HD, TLOC], BF16)
    vx = dint("vx", [TLOC, KV * HD], BF16)
    stl = dint("stl", [KV, 128, 129])
    I32 = mybir.dt.int32
    rk = din("rk", [1, 8], I32)
    tokin = din("tokin", [1, 8], I32)
    sh_st_h = nc.dram_tensor("sh_st", [2 * KV * 128, 129], F32, kind="Internal",
                             addr_space="Shared")
    sh_kx_h = nc.dram_tensor("sh_kx", [2 * KV * HD, TLOC], BF16, kind="Internal",
                             addr_space="Shared")
    sh_vx_h = nc.dram_tensor("sh_vx", [2 * TLOC, KV * HD], BF16, kind="Internal",
                             addr_space="Shared")
    sh_fl_h = nc.dram_tensor("sh_fl", [4, 8], I32, kind="Internal",
                             addr_space="Shared")
    kxg = sh_kx_h.ap()
    vxg = sh_vx_h.ap()
    sh_st = sh_st_h.ap()
    sh_fl = sh_fl_h.ap()
    out = dout("out", [TLOC, D])

    dbg_t = {}
    for name, (shape, dt) in debug.items():
        dbg_t[name] = dout("dbg_" + name, shape, dt)

    RG = [[2 * i, 2 * i + 1] for i in range(n_pairs)]

    stack = contextlib.ExitStack()
    with stack:
        P = Prog(nc, stack)

        def sb(name, shape, dt, st=stack):
            return st.enter_context(nc.sbuf_tensor(name, shape, dt))

        def ps(name, shape, dt, st):
            return st.enter_context(nc.psum_tensor(name, shape, dt))

        yT = sb("yT", [128, DT * TLOC], BF16)
        ident_sb = sb("ident_sb", [128, 128], BF16)
        id32_sb = sb("id32_sb", [128, 128], F32)
        AWB = {}
        e4_sb = sb("e4_sb", [NE, ER], BF16)
        ones_sb = sb("ones_sb", [128, 1], BF16)
        bq_sb = sb("bq_sb", [128, DT], F32)
        bk_sb = sb("bk_sb", [128, KV], F32)
        bv_sb = sb("bv_sb", [128, KV], F32)
        fc_sb = sb("fc_sb", [128, 8], F32)
        rks_sb = sb("rks_sb", [1, 8], mybir.dt.int32)
        toks_sb = sb("toks_sb", [1, 8], mybir.dt.int32)
        for i, v in enumerate(FCONST_VALS):
            nc.const_aps.register(F32, v, fc_sb[:, i:i + 1])
        bns = [sb(f"bns{i}", [128, 48], F32) for i in range(2)]
        mv = [sb(f"mv{i}", [128, 2], F32) for i in range(2)]
        rstd = [sb(f"rstd{i}", [128, 1], F32) for i in range(2)]
        nmr = [sb(f"nmr{i}", [128, 1], F32) for i in range(2)]
        WB = {"b": None}
        XB = {"xt": None, "ybf": None}

        TP = {"banks": None}
        PJ = {"banks": None}

        def transpose_pack(src_slices, dst_ap_fn):
            n8 = len(src_slices)
            bank = P.cur("tpf") % 2
            if P.cur("tpf") >= 2:
                P.wait("tensor", "tpe", P.cur("tpf") - 1)
            tp = TP["banks"][bank]
            for j, s in enumerate(src_slices):
                P.op("tensor", "transpose", tp[:, j * 128:(j + 1) * 128], s, ident_sb[:],
                     inc=("tpf", 1) if j == n8 - 1 else None)
            P.wait("vector", "tpf", P.cur("tpf"))
            P.op("vector", "tensor_copy", dst_ap_fn(n8),
                 tp[:].rearrange("p (a f) -> p a f", f=128)[:, 0:n8, :],
                 inc=("tpe", 1))
            return P.cur("tpe")

        def transpose_tile_to_yt(src_bf, tile_k):
            for pk in range(4):
                srcs = [src_bf[:, (pk * 8 + j) * 128:(pk * 8 + j + 1) * 128] for j in range(8)]

                def dst(n8, pk=pk):
                    return yT[:].rearrange("p (a t) -> p a t", a=DT)[
                        :, pk * 8:pk * 8 + n8, tile_k * 128:(tile_k + 1) * 128]
                transpose_pack(srcs, dst)
            return P.cur("tpe")

        def tok_transpose(src_bf, dst_sb):
            for pk in range((nt + 7) // 8):
                n8 = min(8, nt - pk * 8)
                srcs = [src_bf[:, (pk * 8 + j) * 128:(pk * 8 + j + 1) * 128] for j in range(n8)]

                def dst(n, pk=pk):
                    return dst_sb[:, pk * 8 * 128:(pk * 8 + n) * 128]
                transpose_pack(srcs, dst)
            return P.cur("tpe")

        def ln_tile(xt_ap, b):
            nch = D // 512
            for j in range(nch):
                P.op("vector", "bn_stats", bns[b][:, j * 6:(j + 1) * 6],
                     xt_ap[:, j * 512:(j + 1) * 512])
            P.drain("vector")
            P.op("vector", "bn_aggr", mv[b][:], bns[b][:, 0:6 * nch], inc=("lnv", 1))
            vv = P.cur("lnv")
            P.wait("scalar", "lnv", vv)
            P.op("scalar", "activation", rstd[b][:], mv[b][:, 1:2], AF.Ln, bias=EPS_LN)
            P.drain("scalar")
            P.op("scalar", "activation", rstd[b][:], rstd[b][:], AF.Exp, scale=-0.5,
                 inc=("lns", 1))
            sv = P.cur("lns")
            P.wait("vector", "lns", sv)
            P.op("vector", "scalar_tensor_tensor", nmr[b][:], mv[b][:, 0:1], -1.0, rstd[b][:],
                 ALU.mult, ALU.mult, inc=("lnv", 1))
            return P.cur("lnv")

        def proj_block(wsrc, col0, nbank_tc, yt_wait=None):
            wblk = WB["b"]
            nproj = P.cur("wl")
            bw = nproj % 2
            P.inc("wl")
            if nproj >= 2:
                P.wait("gpsimd", "pjf", (nproj - 1) * TC)
            wv = P.dma("gpsimd", wblk[bw][:].rearrange("p (a f) -> p a f", a=DT),
                       wsrc[:, col0:col0 + 128].rearrange("(a p) f -> p a f", p=128), "wd")
            P.wait("tensor", "wd", wv)
            if yt_wait is not None:
                P.wait("tensor", yt_wait[0], yt_wait[1])
            prj = PJ["banks"]
            for tc in range(TC):
                if nproj >= 1:
                    P.wait("tensor", "pje", (nproj - 1) * TC + tc + 1)
                for dt_i in range(DT):
                    P.op("tensor", "matmul", prj[tc][:],
                         wblk[bw][:, dt_i * 128:(dt_i + 1) * 128],
                         yT[:, dt_i * TLOC + tc * 512:dt_i * TLOC + (tc + 1) * 512],
                         start=(dt_i == 0), stop=(dt_i == DT - 1),
                         inc=(("pjf", 1) if dt_i == DT - 1 else None))

        def mk_copy_to_shared(dst_ap_by_rank, src_ap, flag_row_base, wait_vals):
            v_copy, v_flag = wait_vals

            def fn(g):
                rr = g.alloc_register()
                dsh = P.sem_handles["dsh"]
                g.reg_load(rr, rks_sb[0:1, 0:1])
                with g.If_eq(rr, 0):
                    g.dma_start(out=dst_ap_by_rank[0], in_=src_ap).then_inc(dsh, 16)
                with g.Else():
                    g.dma_start(out=dst_ap_by_rank[1], in_=src_ap).then_inc(dsh, 16)
                g.wait_ge(dsh, v_copy)
                with g.If_eq(rr, 0):
                    g.dma_start(out=sh_fl[flag_row_base:flag_row_base + 1, :],
                                in_=toks_sb[0:1, :]).then_inc(dsh, 16)
                with g.Else():
                    g.dma_start(out=sh_fl[flag_row_base + 1:flag_row_base + 2, :],
                                in_=toks_sb[0:1, :]).then_inc(dsh, 16)
                g.wait_ge(dsh, v_flag)
            return fn

        def mk_poll(flag_row_base):
            def fn(sy):
                tkr = sy.alloc_register()
                pm0 = sy.alloc_register()
                pm1 = sy.alloc_register()
                pc = sy.alloc_register()
                pc2 = sy.alloc_register()
                sy.reg_load(tkr, toks_sb[0:1, 0:1])

                def check():
                    sy.reg_load(pm0, sh_fl[flag_row_base:flag_row_base + 1, 0:1])
                    sy.reg_load(pm1, sh_fl[flag_row_base + 1:flag_row_base + 2, 0:1])
                    sy.reg_alu(pc, pm0, tkr, ALU.bitwise_xor)
                    sy.reg_alu(pc2, pm1, tkr, ALU.bitwise_xor)
                    sy.reg_alu(pc, pc, pc2, ALU.bitwise_or)
                check()
                with sy.While(pc):
                    sy.nop(cycle_cnt=4096)
                    check()
            return fn

        with nc.Block() as block:
            P.dma("sync", ident_sb[:], identb, "dC")
            P.dma("sync", id32_sb[:], ident32, "dC")
            P.dma("sync", fc_sb[:], fconst, "dC")
            P.dma("sync", rks_sb[:], rk, "dC")
            P.dma("sync", toks_sb[:], tokin, "dC")
            P.dma("sync", bq_sb[:], bqT, "dC")
            P.dma("sync", bk_sb[:], bkT, "dC")
            P.dma("sync", bv_sb[:], bvT, "dC")
            P.dma("gpsimd", e4_sb[:], e4, "dCg")
            P.dma("gpsimd", ones_sb[:], onesb, "dCg")
            cAll = P.cur("dC")
            cAllg = P.cur("dCg")
            for e in ENGINES:
                P.wait(e, "dC", cAll)
                P.wait(e, "dCg", cAllg)

            with contextlib.ExitStack() as ph:
                tp1 = contextlib.ExitStack()
                p1buf = contextlib.ExitStack()
                XB["xt"] = [sb(f"xtP{i}", [128, D], F32, p1buf) for i in range(2)]
                XB["ybf"] = [sb(f"ybfP{i}", [128, D], BF16, p1buf) for i in range(2)]
                TP["banks"] = [ps(f"tpP1_{i}", [128, 1024], BF16, tp1) for i in range(2)]

                ytv_hist = []
                def ln_to_yt(src_dram, k, tag):
                    xt = XB["xt"]; ybf = XB["ybf"]
                    b = k % 2
                    if k >= 2:
                        P.wait("sync", f"nrm{tag}", k - 1)
                    lv = P.dma("sync", xt[b][:], src_dram[k * 128:(k + 1) * 128, :], f"dx{tag}")
                    P.wait("vector", f"dx{tag}", lv)
                    if k >= 2:
                        P.wait("vector", f"nrm{tag}", k - 1)
                    ln_tile(xt[b], b)
                    vv = P.cur("lnv")
                    P.wait("scalar", "lnv", vv)
                    if k >= 2:
                        P.wait("scalar", "tpe", ytv_hist[k - 2])
                    P.op("scalar", "activation", ybf[b][:], xt[b][:], AF.Identity,
                         bias=nmr[b][:, 0:1], scale=rstd[b][:, 0:1], inc=(f"nrm{tag}", 1))
                    P.wait("tensor", f"nrm{tag}", k + 1)
                    ytv_hist.append(transpose_tile_to_yt(ybf[b], k))

                for k in range(nt):
                    ln_to_yt(x0, k, "P1")
                yt1_done = ("tpe", P.cur("tpe"))

                P.barrier()
                p1buf.close()

                with contextlib.ExitStack() as pha:
                    WB["b"] = [sb(f"wblkA{i}", [128, DT * 128], BF16, pha) for i in range(2)]
                    PJ["banks"] = [ps(f"pjA_{i}", [128, 512], F32, pha) for i in range(TC)]
                    sk_ps = ps("sk_ps", [128, 132], F32, pha)
                    kft = sb("kft", [128, TLOC], BF16, pha)
                    vft = sb("vft", [128, TLOC], BF16, pha)
                    ktok = sb("ktok", [128, TLOC], BF16, pha)
                    vtok = sb("vtok", [128, TLOC], BF16, pha)
                    tmp1 = sb("tmp1", [128, 512], F32, pha)
                    tmp2 = sb("tmp2", [128, 512], F32, pha)
                    stage = sb("stage", [128, KV * 129], F32, pha)

                    def phi_evac(dst_bf, bias_col):
                        prj = PJ["banks"]
                        base = P.cur("pjf") - TC
                        for tc in range(TC):
                            sl = slice(tc * 512, (tc + 1) * 512)
                            P.wait("vector", "pjf", base + tc + 1)
                            P.drain("vector")
                            P.op("vector", "tensor_scalar", tmp1[:], prj[tc][:], bias_col, 0.0,
                                 ALU.add, ALU.min)
                            P.op("vector", "tensor_scalar", tmp2[:], prj[tc][:], bias_col, 0.0,
                                 ALU.add, ALU.max, inc=("phv", 1))
                            P.wait("scalar", "phv", P.cur("phv"))
                            P.op("scalar", "activation", tmp1[:], tmp1[:], AF.Exp, inc=("phs", 1))
                            P.wait("vector", "phs", P.cur("phs"))
                            P.op("vector", "tensor_tensor", dst_bf[:, sl], tmp2[:], tmp1[:],
                                 ALU.add, inc=("pje", 1))

                    def plain_evac(dst_bf, bias_col):
                        prj = PJ["banks"]
                        base = P.cur("pjf") - TC
                        for tc in range(TC):
                            sl = slice(tc * 512, (tc + 1) * 512)
                            P.wait("vector", "pjf", base + tc + 1)
                            if bias_col is None:
                                P.op("vector", "tensor_copy", dst_bf[:, sl], prj[tc][:],
                                     inc=("pje", 1))
                            else:
                                P.op("vector", "tensor_scalar", dst_bf[:, sl], prj[tc][:],
                                     bias_col, None, ALU.add, inc=("pje", 1))

                    ktv = vtv = 0
                    for g in range(KV):
                        if g > 0:
                            P.wait("vector", "tpe", ktv)
                        proj_block(wkT, g * 128, TC, yt_wait=yt1_done if g == 0 else None)
                        phi_evac(kft, bk_sb[:, g:g + 1])
                        P.wait("tensor", "pje", P.cur("pje"))
                        ktv = tok_transpose(kft, ktok)
                        if g > 0:
                            P.wait("vector", "tpe", vtv)
                        proj_block(wvT, g * 128, TC)
                        plain_evac(vft, bv_sb[:, g:g + 1])
                        P.wait("tensor", "pje", P.cur("pje"))
                        vtv = tok_transpose(vft, vtok)
                        P.wait("tensor", "tpe", P.cur("tpe"))
                        if g > 0:
                            P.wait("tensor", "ste", g)
                        for tt in range(nt):
                            P.op("tensor", "matmul", sk_ps[:, 0:128],
                                 ktok[:, tt * 128:(tt + 1) * 128],
                                 vtok[:, tt * 128:(tt + 1) * 128],
                                 start=(tt == 0), stop=(tt == nt - 1))
                        for tt in range(nt):
                            P.op("tensor", "matmul", sk_ps[:, 128:129],
                                 ktok[:, tt * 128:(tt + 1) * 128],
                                 ones_sb[:], start=(tt == 0), stop=(tt == nt - 1),
                                 inc=([("stf", 1)] if tt == nt - 1 else None))
                        P.wait("vector", "stf", P.cur("stf"))
                        P.op("vector", "tensor_copy", stage[:, g * 129:g * 129 + 129],
                             sk_ps[:, 0:129], inc=("ste", 1))
                    P.wait("sync", "ste", KV)
                    P.dma("sync", stl.rearrange("g p c -> p g c"),
                          stage[:].rearrange("p (g c) -> p g c", g=KV), "dstl")

                    P.wait("gpsimd", "dstl", P.cur("dstl"))
                    P.sem("dsh")
                    wvA = (P.inc("dsh", 16), P.inc("dsh", 16))
                    P.raw("gpsimd", mk_copy_to_shared(
                        [sh_st[0:KV * 128, :], sh_st[KV * 128:2 * KV * 128, :]],
                        stl.rearrange("g p c -> (g p) c"), 0, wvA))

                P.barrier()
                tp1.close()

                with contextlib.ExitStack() as pha:
                    WB["b"] = [sb(f"wblkQ{i}", [128, DT * 128], BF16, pha) for i in range(2)]
                    PJ["banks"] = [ps(f"pjQ_{i}", [128, 512], F32, pha) for i in range(TC)]
                    at_ps = [ps(f"at_{i}", [128, 512], F32, pha) for i in range(2)]
                    phiq = sb("phiq", [128, TLOC], BF16, pha)
                    afst = sb("afst", [128, TLOC], BF16, pha)
                    tmp1 = sb("tmp1a", [128, 512], F32, pha)
                    tmp2 = sb("tmp2a", [128, 512], F32, pha)
                    state_sb = sb("state_sb", [128, KV * 129], F32, pha)
                    st_bf = sb("st_bf", [128, KV * 128], BF16, pha)
                    rec_sb = sb("rec_sb", [128, KV], F32, pha)

                    state_b2 = sb("state_b2", [128, KV * 129], F32, pha)
                    P.raw("sync", mk_poll(0))
                    P.dma("sync", state_sb[:].rearrange("p (g c) -> p g c", g=KV),
                          sh_st[0:KV * 128, :].rearrange("(g p) c -> p g c", p=128),
                          "dstr")
                    sv = P.dma("sync", state_b2[:].rearrange("p (g c) -> p g c", g=KV),
                               sh_st[KV * 128:2 * KV * 128, :]
                               .rearrange("(g p) c -> p g c", p=128), "dstr")
                    P.wait("vector", "dstr", sv)
                    P.op("vector", "tensor_tensor", state_sb[:], state_sb[:],
                         state_b2[:], ALU.add)
                    P.drain("vector")
                    stv = state_sb[:].rearrange("p (g c) -> p g c", g=KV)
                    P.op("vector", "tensor_copy",
                         st_bf[:].rearrange("p (g c) -> p g c", g=KV), stv[:, :, 0:128])
                    P.op("vector", "tensor_scalar",
                         rec_sb[:].rearrange("p (g o) -> p g o", o=1), stv[:, :, 128:129],
                         EPS_LA, None, ALU.add)
                    P.drain("vector")
                    P.op("vector", "reciprocal", rec_sb[:], rec_sb[:], inc=("strdy", 1))
                    if "state" in dbg_t:
                        P.wait("sync", "strdy", 1)
                        P.dma("sync", dbg_t["state"], state_sb[:], "ddbg")

                    P.wait("tensor", "strdy", 1)

                    def phi_evac_q(dst_bf, bias_col):
                        prj = PJ["banks"]
                        base = P.cur("pjf") - TC
                        for tc in range(TC):
                            sl = slice(tc * 512, (tc + 1) * 512)
                            P.wait("vector", "pjf", base + tc + 1)
                            P.drain("vector")
                            P.op("vector", "tensor_scalar", tmp1[:], prj[tc][:], bias_col, 0.0,
                                 ALU.add, ALU.min)
                            P.op("vector", "tensor_scalar", tmp2[:], prj[tc][:], bias_col, 0.0,
                                 ALU.add, ALU.max, inc=("phv", 1))
                            P.wait("scalar", "phv", P.cur("phv"))
                            P.op("scalar", "activation", tmp1[:], tmp1[:], AF.Exp, inc=("phs", 1))
                            P.wait("vector", "phs", P.cur("phs"))
                            P.op("vector", "tensor_tensor", dst_bf[:, sl], tmp2[:], tmp1[:],
                                 ALU.add, inc=("pje", 1))

                    for h in range(HEADS):
                        g = h // GQ
                        if h > 0:
                            P.wait("vector", "atf", TC * h)
                        proj_block(wqT, h * 128, TC)
                        phi_evac_q(phiq, bq_sb[:, h:h + 1])
                        P.wait("tensor", "pje", P.cur("pje"))
                        if h > 0:
                            P.wait("vector", "dafa", 16 * h)
                        for tc in range(TC):
                            bb = tc % 2
                            n_at = TC * h + tc
                            if n_at >= 2:
                                P.wait("tensor", "ate", n_at - 1)
                            P.op("tensor", "matmul", at_ps[bb][:],
                                 st_bf[:, g * 128:(g + 1) * 128],
                                 phiq[:, tc * 512:(tc + 1) * 512],
                                 start=True, stop=True, inc=("atf", 1))
                            P.wait("vector", "atf", P.cur("atf"))
                            P.op("vector", "tensor_scalar", afst[:, tc * 512:(tc + 1) * 512],
                                 at_ps[bb][:], rec_sb[:, g:g + 1], None, ALU.mult,
                                 inc=("ate", 1))
                        P.wait("sync", "ate", P.cur("ate"))
                        P.dma("sync", afa[h * 128:(h + 1) * 128, :], afst[:], "dafa")
                    P.wait("sync", "dafa", P.cur("dafa"))

                P.barrier()

                def out_proj(wsrc, af_in_yt, src_x, dst_x, add_bo, tag):
                    GS = min(8, nt)
                    NGrp = nt // GS
                    with contextlib.ExitStack() as pho:
                        ob = [ps(f"ob{tag}_{i}", [128, 512], F32, pho) for i in range(GS)]
                        wr2 = [sb(f"wr2{tag}_{i}", [128, DT * 256], BF16, pho) for i in range(2)]
                        xl = sb(f"xl{tag}", [128, 8 * 512], F32, pho)
                        xs = sb(f"xs{tag}", [128, 8 * 512], F32, pho)
                        bo_st = sb(f"bo{tag}", [128, 512], F32, pho) if add_bo else None
                        for grp in range(NGrp):
                            for fch in range(8):
                                it = grp * 8 + fch
                                if add_bo:
                                    P.wait("sync", f"pe{tag}", it)
                                    P.dma("sync", bo_st[:], bo_bc[:, fch * 512:(fch + 1) * 512],
                                          f"dbo{tag}")
                                P.wait("sync", f"pe{tag}", it)
                                P.dma("sync", xl[:, 0:GS * 512].rearrange("p (a f) -> p a f", a=GS),
                                      src_x.rearrange("(a p) d -> p a d", p=128)
                                      [:, grp * GS:(grp + 1) * GS, fch * 512:(fch + 1) * 512],
                                      f"dxl{tag}")
                                for half in range(2):
                                    nw = it * 2 + half
                                    bw = nw % 2
                                    if it >= 1:
                                        P.wait("gpsimd", f"wh{half}{tag}", it)
                                    P.dma("gpsimd", wr2[bw][:].rearrange("p (a f) -> p a f", a=DT),
                                          wsrc[:, fch * 512 + half * 256:
                                               fch * 512 + (half + 1) * 256]
                                          .rearrange("(a p) f -> p a f", p=128), f"wl{tag}")
                                    P.wait("tensor", f"wl{tag}", 16 * (nw + 1))
                                    if half == 0 and it >= 1:
                                        P.wait("tensor", f"pe{tag}", it)
                                    for ct in range(DT):
                                        for tt in range(GS):
                                            last = (ct == DT - 1 and tt == GS - 1)
                                            P.op("tensor", "matmul",
                                                 ob[tt][:, half * 256:(half + 1) * 256],
                                                 yT[:, ct * TLOC + (grp * GS + tt) * 128:
                                                    ct * TLOC + (grp * GS + tt + 1) * 128],
                                                 wr2[bw][:, ct * 256:(ct + 1) * 256],
                                                 start=(ct == 0), stop=(ct == DT - 1),
                                                 inc=((f"wh{half}{tag}", 1) if last else None))
                                P.wait("vector", f"wh1{tag}", it + 1)
                                P.wait("vector", f"dxl{tag}", 16 * (it + 1))
                                if add_bo:
                                    P.wait("vector", f"dbo{tag}", 16 * (it + 1))
                                if it >= 1:
                                    P.wait("vector", f"sd{tag}", 16 * it)
                                for tt in range(GS):
                                    sl = slice(tt * 512, (tt + 1) * 512)
                                    last = ("pe" + tag, 1) if (tt == GS - 1 and not add_bo) else None
                                    P.op("vector", "tensor_tensor", xs[:, sl], ob[tt][:],
                                         xl[:, sl], ALU.add, inc=last)
                                    if add_bo:
                                        P.drain("vector")
                                        P.op("vector", "tensor_tensor", xs[:, sl], xs[:, sl],
                                             bo_st[:], ALU.add,
                                             inc=(("pe" + tag, 1) if tt == GS - 1 else None))
                                P.wait("sync", f"pe{tag}", it + 1)
                                P.dma("sync", dst_x.rearrange("(a p) d -> p a d", p=128)
                                      [:, grp * GS:(grp + 1) * GS, fch * 512:(fch + 1) * 512],
                                      xs[:, 0:GS * 512].rearrange("p (a f) -> p a f", a=GS),
                                      f"sd{tag}")
                        P.wait("sync", f"sd{tag}", 16 * NGrp * 8)

                lv = P.dma("sync", yT[:].rearrange("p (a t) -> p a t", a=DT),
                           afa.rearrange("(a p) t -> p a t", p=128), "dafl")
                P.wait("tensor", "dafl", lv)
                out_proj(woT, True, x0, x1d, True, "A4")

                P.barrier()

                def ffn_epilogue(src_dram, dst_dram, tag, build_next_yt):
                    eytv = []
                    with contextlib.ExitStack() as phe:
                        xt = XB["xt"] = [sb(f"xt{tag}{i}", [128, D], F32, phe) for i in range(2)]
                        ybf = XB["ybf"] = [sb(f"ybf{tag}{i}", [128, D], BF16, phe)
                                           for i in range(2)]
                        awr_sb = AWB["awr"] = sb(f"awr{tag}", [128, DT * (ER + NE)], BF16, phe)
                        ball_sb = AWB["ball"] = sb(f"ball{tag}", [ER, D], BF16, phe)
                        P.dma("gpsimd", awr_sb[:].rearrange("p (a c) -> p a c", a=DT),
                              awr.rearrange("a p c -> p a c"), "dCg")
                        P.dma("gpsimd", ball_sb[:], ballT, "dCg")
                        awbv = P.cur("dCg")
                        P.wait("tensor", "dCg", awbv)
                        TP["banks"] = [ps(f"tp{tag}_{i}", [128, 1024], BF16, phe)
                                       for i in range(2)]
                        dl_ps = [ps(f"dl{tag}_{i}", [128, 512], F32, phe) for i in range(2)]
                        sm_ps = ps(f"sm{tag}", [128, 512], F32, phe)
                        mid_ps = sm_ps[0:ER + NE, 0:128]
                        lgt_ps = sm_ps[:, 128:132]
                        gt_ps = sm_ps[0:NE, 132:260]
                        grep_ps = sm_ps[0:ER, 260:388]
                        msb = sb(f"msb{tag}", [ER + NE, 128], F32, phe)
                        lgt_sb = sb(f"lgt{tag}", [128, NE], F32, phe)
                        gsc = sb(f"gsc{tag}", [128, 8], F32, phe)
                        pe_sb = sb(f"pes{tag}", [128, NE], F32, phe)
                        gmask = sb(f"gmk{tag}", [128, NE], F32, phe)
                        g_sb = sb(f"gsb{tag}", [128, NE], F32, phe)
                        gt_sb = sb(f"gts{tag}", [NE, 128], BF16, phe)
                        gmid_sb = sb(f"gmd{tag}", [ER, 128], BF16, phe)

                        for k in range(nt):
                            b = k % 2
                            if k >= 2:
                                P.wait("sync", f"dle{tag}", 8 * (k - 1))
                                P.wait("sync", f"xst{tag}", 16 * (k - 1))
                                if build_next_yt:
                                    P.wait("sync", f"nr2{tag}", k - 1)
                            lv = P.dma("sync", xt[b][:], src_dram[k * 128:(k + 1) * 128, :],
                                       f"dx{tag}")
                            P.wait("vector", f"dx{tag}", lv)
                            if k >= 2:
                                P.wait("vector", f"nrm{tag}", k - 1)
                            ln_tile(xt[b], b)
                            P.wait("scalar", "lnv", P.cur("lnv"))
                            if k >= 2:
                                P.wait("scalar", "tpe", eytv[2 * (k - 2) + 1])
                            P.op("scalar", "activation", ybf[b][:], xt[b][:], AF.Identity,
                                 bias=nmr[b][:, 0:1], scale=rstd[b][:, 0:1],
                                 inc=(f"nrm{tag}", 1))
                            P.wait("tensor", f"nrm{tag}", k + 1)
                            ytv = transpose_tile_to_yt(ybf[b], k)
                            eytv.append(ytv)
                            P.wait("tensor", "tpe", ytv)
                            if k > 0:
                                P.wait("tensor", f"mde{tag}", k)
                            for dt_i in range(DT):
                                P.op("tensor", "matmul", mid_ps,
                                     awr_sb[:, dt_i * (ER + NE):(dt_i + 1) * (ER + NE)],
                                     yT[:, dt_i * TLOC + k * 128:dt_i * TLOC + (k + 1) * 128],
                                     start=(dt_i == 0), stop=(dt_i == DT - 1),
                                     inc=([(f"mdf{tag}", 1)] if dt_i == DT - 1 else None))
                            P.wait("vector", f"mdf{tag}", k + 1)
                            if k > 0:
                                P.wait("vector", f"lgf{tag}", k)
                            P.op("vector", "tensor_copy", msb[:], mid_ps, inc=(f"mde{tag}", 1))
                            P.wait("tensor", f"mde{tag}", k + 1)
                            P.op("tensor", "transpose", lgt_ps, msb[ER:ER + NE, :],
                                 id32_sb[ER:ER + NE, ER:ER + NE], inc=(f"lgf{tag}", 1))
                            P.wait("scalar", f"lgf{tag}", k + 1)
                            P.op("scalar", "copy", lgt_sb[:], lgt_ps, inc=(f"lge{tag}", 1))
                            P.wait("vector", f"lge{tag}", k + 1)
                            l0, l1 = lgt_sb[:, 0:1], lgt_sb[:, 1:2]
                            l2, l3 = lgt_sb[:, 2:3], lgt_sb[:, 3:4]
                            P.op("vector", "tensor_tensor", gsc[:, 0:1], l0, l1, ALU.max)
                            P.op("vector", "tensor_tensor", gsc[:, 1:2], l0, l1, ALU.min)
                            P.op("vector", "tensor_tensor", gsc[:, 2:3], l2, l3, ALU.max)
                            P.op("vector", "tensor_tensor", gsc[:, 3:4], l2, l3, ALU.min)
                            P.drain("vector")
                            P.op("vector", "tensor_tensor", gsc[:, 4:5], gsc[:, 0:1],
                                 gsc[:, 2:3], ALU.max)
                            P.op("vector", "tensor_tensor", gsc[:, 5:6], gsc[:, 0:1],
                                 gsc[:, 2:3], ALU.min)
                            P.op("vector", "tensor_tensor", gsc[:, 6:7], gsc[:, 1:2],
                                 gsc[:, 3:4], ALU.max)
                            P.drain("vector")
                            P.op("vector", "tensor_tensor", gsc[:, 5:6], gsc[:, 5:6],
                                 gsc[:, 6:7], ALU.max, inc=(f"gmx{tag}", 1))
                            P.wait("scalar", f"gmx{tag}", k + 1)
                            P.op("scalar", "mul", gsc[:, 7:8], gsc[:, 4:5], -1.0)
                            P.drain("scalar")
                            P.op("scalar", "activation", pe_sb[:], lgt_sb[:], AF.Exp,
                                 bias=gsc[:, 7:8], inc=(f"pex{tag}", 1))
                            P.wait("vector", f"pex{tag}", k + 1)
                            P.op("vector", "tensor_scalar", gmask[:], lgt_sb[:], gsc[:, 5:6],
                                 None, ALU.is_ge)
                            P.drain("vector")
                            P.op("vector", "tensor_tensor", g_sb[:], pe_sb[:], gmask[:],
                                 ALU.mult)
                            P.drain("vector")
                            P.op("vector", "reduce_sum", gsc[:, 6:7], g_sb[:], axis=AX)
                            P.drain("vector")
                            P.op("vector", "reciprocal", gsc[:, 7:8], gsc[:, 6:7])
                            P.drain("vector")
                            P.op("vector", "tensor_scalar", g_sb[:], g_sb[:], gsc[:, 7:8],
                                 None, ALU.mult, inc=(f"gv{tag}", 1))
                            P.wait("tensor", f"gv{tag}", k + 1)
                            P.op("tensor", "transpose", gt_ps, g_sb[:], id32_sb[:],
                                 inc=(f"gtf{tag}", 1))
                            P.wait("scalar", f"gtf{tag}", k + 1)
                            P.op("scalar", "copy", gt_sb[:], gt_ps, inc=(f"gte{tag}", 1))
                            P.wait("tensor", f"gte{tag}", k + 1)
                            P.op("tensor", "matmul", grep_ps, e4_sb[:], gt_sb[:],
                                 start=True, stop=True, inc=(f"grf{tag}", 1))
                            P.wait("vector", f"grf{tag}", k + 1)
                            P.op("vector", "tensor_tensor", gmid_sb[:], grep_ps, msb[0:ER, :],
                                 ALU.mult, inc=(f"gmv{tag}", 1))
                            P.wait("tensor", f"gmv{tag}", k + 1)
                            for fch in range(8):
                                bb = fch % 2
                                ndl = 8 * k + fch
                                if ndl >= 2:
                                    P.wait("tensor", f"dle{tag}", ndl - 1)
                                P.op("tensor", "matmul", dl_ps[bb][:], gmid_sb[:],
                                     ball_sb[:, fch * 512:(fch + 1) * 512],
                                     start=True, stop=True, inc=(f"dlf{tag}", 1))
                                P.wait("vector", f"dlf{tag}", P.cur(f"dlf{tag}"))
                                P.op("vector", "scalar_tensor_tensor",
                                     xt[b][:, fch * 512:(fch + 1) * 512], dl_ps[bb][:],
                                     SCALE_FFN, xt[b][:, fch * 512:(fch + 1) * 512],
                                     ALU.mult, ALU.add, inc=(f"dle{tag}", 1))
                            P.wait("sync", f"dle{tag}", 8 * (k + 1))
                            P.dma("sync", dst_dram[k * 128:(k + 1) * 128, :], xt[b][:],
                                  f"xst{tag}")
                            if build_next_yt:
                                P.drain("vector")
                                ln_tile(xt[b], b)
                                P.wait("scalar", "lnv", P.cur("lnv"))
                                P.wait("scalar", "tpe", ytv)
                                P.op("scalar", "activation", ybf[b][:], xt[b][:], AF.Identity,
                                     bias=nmr[b][:, 0:1], scale=rstd[b][:, 0:1],
                                     inc=(f"nr2{tag}", 1))
                                P.wait("tensor", f"nr2{tag}", k + 1)
                                eytv.append(transpose_tile_to_yt(ybf[b], k))
                            else:
                                eytv.append(ytv)
                        P.wait("sync", f"xst{tag}", 16 * nt)

                ffn_epilogue(x1d, x2d, "E1", True)
                yt3_done = ("tpe", P.cur("tpe"))
                P.barrier()

                with contextlib.ExitStack() as phb:
                    WB["b"] = [sb(f"wblkB{i}", [128, DT * 128], BF16, phb) for i in range(2)]
                    PJ["banks"] = [ps(f"pjB_{i}", [128, 512], F32, phb) for i in range(TC)]
                    TP["banks"] = [ps(f"tpB_{i}", [128, 1024], BF16, phb) for i in range(2)]
                    cos_sb = sb("cos_sb", [64, TLOC], F32, phb)
                    sin_sb = sb("sin_sb", [64, TLOC], F32, phb)
                    qt_sb = sb("qt_sb", [128, TLOC], BF16, phb)
                    vft2 = sb("vft2", [128, TLOC], BF16, phb)
                    vtok2 = sb("vtok2", [128, TLOC], BF16, phb)
                    r1 = sb("r1", [64, 512], F32, phb)
                    r2 = sb("r2", [64, 512], F32, phb)
                    P.dma("sync", cos_sb[:], cosf, "dC")
                    P.dma("sync", sin_sb[:], sinf, "dC")
                    P.wait("vector", "dC", P.cur("dC"))

                    def rope_evac(dst_bf):
                        prj = PJ["banks"]
                        base = P.cur("pjf") - TC
                        for tc in range(TC):
                            sl = slice(tc * 512, (tc + 1) * 512)
                            cc_ = cos_sb[:, sl]
                            ss_ = sin_sb[:, sl]
                            P.wait("vector", "pjf", base + tc + 1)
                            P.drain("vector")
                            P.op("vector", "tensor_tensor", r1[:], prj[tc][0:64, :], cc_,
                                 ALU.mult)
                            P.op("vector", "tensor_tensor", r2[:], prj[tc][64:128, :], ss_,
                                 ALU.mult)
                            P.drain("vector")
                            P.op("vector", "tensor_tensor", dst_bf[0:64, sl], r1[:], r2[:],
                                 ALU.subtract)
                            P.drain("vector")
                            P.op("vector", "tensor_tensor", r1[:], prj[tc][0:64, :], ss_,
                                 ALU.mult)
                            P.op("vector", "tensor_tensor", r2[:], prj[tc][64:128, :], cc_,
                                 ALU.mult)
                            P.drain("vector")
                            P.op("vector", "tensor_tensor", dst_bf[64:128, sl], r1[:], r2[:],
                                 ALU.add, inc=("pje", 1))

                    P.wait("tensor", yt3_done[0], yt3_done[1])
                    nspill = 0
                    for g in range(KV):
                        if nspill > 0:
                            P.wait("vector", "spl", 16 * nspill)
                        proj_block(wqkv, (DT + g) * 128, TC)
                        rope_evac(qt_sb)
                        P.wait("sync", "pje", P.cur("pje"))
                        P.dma("sync", kx[g * 128:(g + 1) * 128, :], qt_sb[:], "spl")
                        nspill += 1
                    vtv2 = 0
                    for g in range(KV):
                        if g > 0:
                            P.wait("vector", "tpe", vtv2)
                        proj_block(wqkv, (DT + KV + g) * 128, TC)
                        plain = PJ["banks"]
                        vbase = P.cur("pjf") - TC
                        for tc in range(TC):
                            P.wait("vector", "pjf", vbase + tc + 1)
                            P.op("vector", "tensor_copy", vft2[:, tc * 512:(tc + 1) * 512],
                                 plain[tc][:], inc=("pje", 1))
                        P.wait("tensor", "pje", P.cur("pje"))
                        if g > 0:
                            P.wait("tensor", "spl", 16 * nspill)
                        vtv2 = tok_transpose(vft2, vtok2)
                        P.wait("sync", "tpe", P.cur("tpe"))
                        P.dma("sync",
                              vx.rearrange("(a p) f -> p a f", p=128)[:, :, g * 128:(g + 1) * 128],
                              vtok2[:].rearrange("p (a f) -> p a f", a=nt), "spl")
                        nspill += 1
                    P.wait("gpsimd", "spl", 16 * nspill)
                    P.sem("dsh")
                    wvK = (P.inc("dsh", 16), None)
                    wvV = (P.inc("dsh", 16), P.inc("dsh", 16))

                    def stageB(g, wvK=wvK, wvV=wvV):
                        rr = g.alloc_register()
                        dsh = P.sem_handles["dsh"]
                        g.reg_load(rr, rks_sb[0:1, 0:1])
                        with g.If_eq(rr, 0):
                            g.dma_start(out=kxg[0:KV * HD, :], in_=kx).then_inc(dsh, 16)
                            g.dma_start(out=vxg[0:TLOC, :], in_=vx).then_inc(dsh, 16)
                        with g.Else():
                            g.dma_start(out=kxg[KV * HD:2 * KV * HD, :],
                                        in_=kx).then_inc(dsh, 16)
                            g.dma_start(out=vxg[TLOC:2 * TLOC, :],
                                        in_=vx).then_inc(dsh, 16)
                        g.wait_ge(dsh, wvV[0])
                        with g.If_eq(rr, 0):
                            g.dma_start(out=sh_fl[2:3, :],
                                        in_=toks_sb[0:1, :]).then_inc(dsh, 16)
                        with g.Else():
                            g.dma_start(out=sh_fl[3:4, :],
                                        in_=toks_sb[0:1, :]).then_inc(dsh, 16)
                        g.wait_ge(dsh, wvV[1])
                    P.raw("gpsimd", stageB)
                    P.wait("sync", "spl", 16 * nspill)

                P.barrier()

                with contextlib.ExitStack() as phc:
                    s_ps4 = [ps(f"s4_{i}", [128, 512], F32, phc) for i in range(3)]
                    fq_ps = ps("fq_ps", [128, 512], F32, phc)
                    pt_ps2 = [ps(f"pt2_{i}", [128, 512], F32, phc) for i in range(2)]
                    ov_ps2 = [ps(f"ov_ps{i}", [128, 512], F32, phc) for i in range(2)]
                    mask_sb = sb("mask_sb", [128, 1024], BF16, phc)
                    kg_sb = sb("kg_sb", [128, 2 * TLOC], BF16, phc)
                    vg_sb = sb("vg_sb", [128, 2 * TLOC], BF16, phc)
                    S_sb = sb("S_sb", [128, 4096], F32, phc)
                    P2 = [sb(f"P2_{i}", [128, 4096], BF16, phc) for i in range(2)]
                    PT_sb = sb("PT_sb", [128, 256 * nt], BF16, phc)
                    qh_sb = [sb(f"qh_{i}", [128, 512], BF16, phc) for i in range(2)]
                    zb2 = [sb(f"zb{i}", [128, NE], F32, phc) for i in range(2)]
                    izc = sb("izc", [128, 1], F32, phc)
                    dz2 = [sb(f"dz{i}", [128, 128], BF16, phc) for i in range(2)]
                    pmx = sb("pmx", [128, 8], F32, phc)
                    mrow = sb("mrow", [128, 1], F32, phc)
                    nmsc = sb("nmsc", [128, 1], F32, phc)
                    wfq = sb("wfq", [128, DT * 128], BF16, phc)
                    cs_c = sb("cs_c", [64, 512], BF16, phc)
                    cs_s = sb("cs_s", [64, 512], BF16, phc)
                    qs1 = sb("qs1", [128, 512], BF16, phc)
                    fr1_t = sb("fr1_t", [64, 512], F32, phc)
                    fr2_t = sb("fr2_t", [64, 512], F32, phc)
                    fr1 = fr1_t[:]
                    fr2 = fr2_t[:]
                    afs2 = [sb(f"afs1_{i}", [128, 512], BF16, phc) for i in range(2)]
                    P.dma("gpsimd", mask_sb[:, 0:512], maskd[0], "dCg")
                    P.dma("gpsimd", mask_sb[:, 512:1024], maskd[1], "dCg")
                    P.wait("vector", "dCg", P.cur("dCg"))
                    P.wait("tensor", yt3_done[0], yt3_done[1])
                    P.raw("sync", mk_poll(2))

                    fill_state = {"h": 0, "tc": 0, "nqs": 0}
                    qspl = {}

                    def filler_step():
                        h = fill_state["h"]
                        if h >= HEADS:
                            return
                        tc = fill_state["tc"]
                        if tc == 0:
                            if h >= 1:
                                P.wait("gpsimd", "fqf", TC * h)
                            P.dma("gpsimd",
                                  wfq[:].rearrange("p (a f) -> p a f", a=DT),
                                  wqkv[:, h * 128:(h + 1) * 128]
                                  .rearrange("(a p) f -> p a f", p=128), "fqw")
                        nqs0 = fill_state["nqs"]
                        P.wait("gpsimd", "fqe", nqs0)
                        P.dma("gpsimd", cs_c[:], cosf[:, tc * 512:(tc + 1) * 512], "fcs")
                        P.dma("gpsimd", cs_s[:], sinf[:, tc * 512:(tc + 1) * 512], "fcs")
                        P.wait("tensor", "fqw", 16 * (h + 1))
                        P.wait("tensor", "fqe", TC * h + tc)
                        for dt_i in range(DT):
                            P.op("tensor", "matmul", fq_ps[:],
                                 wfq[:, dt_i * 128:(dt_i + 1) * 128],
                                 yT[:, dt_i * TLOC + tc * 512:dt_i * TLOC + (tc + 1) * 512],
                                 start=(dt_i == 0), stop=(dt_i == DT - 1),
                                 inc=(("fqf", 1) if dt_i == DT - 1 else None))
                        nqs = fill_state["nqs"]
                        sl = slice(tc * 512, (tc + 1) * 512)
                        P.wait("vector", "fqf", TC * h + tc + 1)
                        P.wait("vector", "fcs", 32 * (nqs + 1))
                        if nqs >= 1:
                            P.wait("vector", "qsd", 16 * nqs)
                        P.drain("vector")
                        P.op("vector", "tensor_tensor", fr1, fq_ps[0:64, :], cs_c[:],
                             ALU.mult)
                        P.op("vector", "tensor_tensor", fr2, fq_ps[64:128, :], cs_s[:],
                             ALU.mult)
                        P.drain("vector")
                        P.op("vector", "tensor_tensor", qs1[0:64, :], fr1, fr2,
                             ALU.subtract)
                        P.drain("vector")
                        P.op("vector", "tensor_tensor", fr1, fq_ps[0:64, :], cs_s[:],
                             ALU.mult)
                        P.op("vector", "tensor_tensor", fr2, fq_ps[64:128, :], cs_c[:],
                             ALU.mult)
                        P.drain("vector")
                        P.op("vector", "tensor_tensor", qs1[64:128, :], fr1, fr2,
                             ALU.add, inc=("fqe", 1))
                        P.wait("sync", "fqe", TC * h + tc + 1)
                        P.dma("sync", qTb[h * 128:(h + 1) * 128, tc * 512:(tc + 1) * 512],
                              qs1[:], "qsd")
                        fill_state["nqs"] = nqs + 1
                        if tc == TC - 1:
                            qspl[h] = P.cur("qsd")
                            fill_state["h"] = h + 1
                            fill_state["tc"] = 0
                        else:
                            fill_state["tc"] = tc + 1

                    for _ in range(4 * TC):
                        filler_step()

                    nhead = 0
                    qfree = []
                    units = [(gh, g2, hh) for gh in range(KV)
                             for g2 in range(NTG) for hh in range(GQ)]
                    qh_v = {}

                    def issue_qh(i):
                        gh_i, g2_i, hh_i = units[i]
                        h_i = gh_i * GQ + hh_i
                        if i >= 2:
                            for bi in range(3):
                                P.wait("sync", f"se{bi}", qfree[i - 2][bi])
                        P.wait("sync", "qsd", qspl[h_i])
                        P.dma("sync", qh_sb[i % 2][:],
                              qTb[h_i * 128:(h_i + 1) * 128,
                                  g2_i * 512:(g2_i + 1) * 512], "dqh")
                        qh_v[i] = P.cur("dqh")

                    issue_qh(0)
                    for ui, (gh, g2, hh) in enumerate(units):
                        if g2 == 0 and hh == 0:
                            P.wait("sync", "pvp", 4 * nhead)
                            for r in range(2):
                                P.dma("sync",
                                      kg_sb[:].rearrange("p (a f) -> p a f", f=256)
                                      [:, 0:nt, r * 128:(r + 1) * 128],
                                      kxg[r * KV * HD + gh * HD:
                                          r * KV * HD + (gh + 1) * HD,
                                          0:nt * 128]
                                      .rearrange("p (a f) -> p a f", f=128),
                                      "dkg")
                                P.dma("sync",
                                      vg_sb[:].rearrange("p (a f) -> p a f", f=256)
                                      [:, 0:nt, r * 128:(r + 1) * 128],
                                      vxg.rearrange("(a p) f -> p a f", p=128)
                                      [:, r * nt:(r + 1) * nt, gh * 128:(gh + 1) * 128],
                                      "dvg")
                            kgv = P.cur("dkg")
                            vgv = P.cur("dvg")
                        if True:
                            if True:
                                h = gh * GQ + hh
                                filler_step()
                                if ui + 1 < len(units):
                                    issue_qh(ui + 1)
                                bq_ = nhead % 2
                                P.wait("tensor", "dqh", qh_v[ui])
                                P.wait("tensor", "dkg", kgv)
                                P.wait("tensor", "dvg", vgv)
                                dzv_v = [0] * 4

                                def scores_part(kk):
                                    kloc = 4 * g2 + kk
                                    nck = kloc // 2 + 1
                                    pb = kk % 2
                                    P.wait("vector", "pex", P.cur("pex"))
                                    ev_info = []
                                    for c in range(nck):
                                        nsc = P.cur("sf")
                                        bs = nsc % 3
                                        if nsc >= 3:
                                            P.wait("tensor", f"se{bs}", nsc // 3)
                                        P.op("tensor", "matmul", s_ps4[bs][:],
                                             qh_sb[bq_][:, kk * 128:(kk + 1) * 128],
                                             kg_sb[:, c * 512:(c + 1) * 512],
                                             start=True, stop=True, inc=("sf", 1))
                                        if c == nck - 1:
                                            P.wait("vector", "sf", P.cur("sf"))
                                            P.op("vector", "tensor_tensor",
                                                 S_sb[:, c * 512:(c + 1) * 512], s_ps4[bs][:],
                                                 mask_sb[:, (kloc % 2) * 512:
                                                         (kloc % 2 + 1) * 512],
                                                 ALU.add, inc=(f"se{bs}", 1))
                                            ev_info.append((c, None, 0))
                                        else:
                                            P.wait("scalar", "sf", P.cur("sf"))
                                            P.op("scalar", "copy",
                                                 S_sb[:, c * 512:(c + 1) * 512], s_ps4[bs][:],
                                                 inc=(f"se{bs}", 1))
                                            ev_info.append((c, bs, P.cur(f"se{bs}")))
                                    sev = [P.cur(f"se{bi}") for bi in range(3)]
                                    for bi in range(3):
                                        P.wait("vector", f"se{bi}", sev[bi])
                                    P.drain("vector")
                                    P.op("vector", "reduce_max", mrow[:], S_sb[:, 0:nck * 512],
                                         axis=AX)
                                    P.drain("vector")
                                    P.op("vector", "tensor_scalar", nmsc[:], mrow[:],
                                         -SM_SCALE, None, ALU.mult, inc=("smv", 1))
                                    P.wait("scalar", "smv", P.cur("smv"))
                                    P.wait("scalar", "pte", P.cur("ptf"))
                                    if kk == 0 and nhead >= 2:
                                        P.wait("scalar", "izv", 4 * (nhead - 1))
                                    P.op("scalar", "activation", P2[pb][:, 0:nck * 512],
                                         S_sb[:, 0:nck * 512], AF.Exp, bias=nmsc[:, 0:1],
                                         scale=SM_SCALE,
                                         accum_out=zb2[nhead % 2][:, kk:kk + 1],
                                         inc=("pex", 1))
                                    P.wait("vector", "pex", P.cur("pex"))
                                    P.drain("vector")
                                    P.op("vector", "reciprocal", izc[:],
                                         zb2[nhead % 2][:, kk:kk + 1], inc=("izv", 1))
                                    P.wait("vector", "pte", P.cur("ptf"))
                                    P.drain("vector")
                                    P.op("vector", "tensor_scalar", dz2[pb][:],
                                         ident_sb[:], izc[:, 0:1], None, ALU.mult,
                                         inc=("dzv", 1))
                                    dzv_v[kk] = P.cur("dzv")

                                def post_part(kk):
                                    kloc = 4 * g2 + kk
                                    nck = kloc // 2 + 1
                                    pb = kk % 2
                                    P.wait("tensor", "dzv", dzv_v[kk])
                                    P.wait("tensor", "pvp", 4 * nhead + kk)
                                    nT = nck * 4
                                    for pk in range((nT + 3) // 4):
                                        n4 = min(4, nT - pk * 4)
                                        srcs = [P2[pb][:, (pk * 4 + j) * 128:
                                                       (pk * 4 + j + 1) * 128]
                                                for j in range(n4)]
                                        bank = P.cur("ptf") % 2
                                        if P.cur("ptf") >= 2:
                                            P.wait("tensor", "pte", P.cur("ptf") - 1)
                                        tp = pt_ps2[bank]
                                        for j in range(n4):
                                            P.op("tensor", "matmul",
                                                 tp[:, j * 128:(j + 1) * 128], srcs[j],
                                                 dz2[pb][:], start=True, stop=True)
                                        meth, args, kw, incs = P.ops["tensor"][-1]
                                        incs.append((P.sem("ptf"), 1))
                                        P.counts["ptf"] += 1
                                        P.wait("scalar", "ptf", P.cur("ptf"))
                                        P.op("scalar", "copy",
                                             PT_sb[:, pk * 4 * 128:(pk * 4 + n4) * 128],
                                             tp[:, 0:n4 * 128], inc=("pte", 1))
                                    P.wait("tensor", "pte", P.cur("pte"))
                                    if kk == 0 and nhead >= 2:
                                        P.wait("tensor", "ove", nhead - 1)
                                    for c in range(nck):
                                        for tt in range(4):
                                            T = 4 * c + tt
                                            last = (c == nck - 1 and tt == 3)
                                            P.op("tensor", "matmul",
                                                 ov_ps2[nhead % 2][:, kk * 128:
                                                                   (kk + 1) * 128],
                                                 vg_sb[:, T * 128:(T + 1) * 128],
                                                 PT_sb[:, T * 128:(T + 1) * 128],
                                                 start=(c == 0 and tt == 0),
                                                 stop=last,
                                                 inc=(("pvp", 1) if last else None))

                                for step in range(5):
                                    if step < 4:
                                        scores_part(step)
                                    if step >= 1:
                                        post_part(step - 1)
                                hb = nhead % 2
                                P.wait("vector", "pvp", P.cur("pvp"))
                                if nhead >= 2:
                                    P.wait("vector", "dab", 16 * (nhead - 1))
                                P.op("vector", "tensor_copy", afs2[nhead % 2][:],
                                     ov_ps2[hb][:], inc=("ove", 1))
                                P.wait("sync", "ove", P.cur("ove"))
                                P.dma("sync", afb[h * 128:(h + 1) * 128,
                                               g2 * 512:(g2 + 1) * 512],
                                      afs2[nhead % 2][:], "dab")
                                qfree.append(tuple(P.cur(f"se{bi}") for bi in range(3)))
                                nhead += 1
                    P.wait("sync", "dab", P.cur("dab"))

                P.barrier()

                lv2 = P.dma("sync", yT[:].rearrange("p (a t) -> p a t", a=DT),
                            afb.rearrange("(a p) t -> p a t", p=128), "dafl")
                P.wait("tensor", "dafl", lv2)
                out_proj(wo2, True, x2d, x3d, False, "B4")
                P.barrier()

                ffn_epilogue(x3d, out, "E2", False)

            P.finalize(block)

    return nc



def _prep_inputs(inputs, nt=16):
    TLOC = nt * 128
    hs = inputs["L0_attn_a_hidden_states"]
    B = hs.shape[0]
    f32 = np.float32
    Wq = inputs["L0_attn_a_Wq"]; bq = inputs["L0_attn_a_bq"]
    Wk = inputs["L0_attn_a_Wk"]; bk = inputs["L0_attn_a_bk"]
    Wv = inputs["L0_attn_a_Wv"]; bv = inputs["L0_attn_a_bv"]
    Wo = inputs["L0_attn_a_Wo"]; bo = inputs["L0_attn_a_bo"]
    wqkv = inputs["L0_attn_b_w_qkv"]; w_o = inputs["L0_attn_b_w_o"]
    A = inputs["L0_ffn_A"]; Bm = inputs["L0_ffn_B"]; Wr = inputs["L0_ffn_Wr"]

    wqT = np.ascontiguousarray(np.asarray(Wq).T, dtype=f32)
    wkT = np.ascontiguousarray(np.asarray(Wk).T, dtype=f32)
    wvT = np.ascontiguousarray(np.asarray(Wv).T, dtype=f32)
    woT = np.ascontiguousarray(np.asarray(Wo).T, dtype=f32)
    wo2 = np.ascontiguousarray(np.asarray(w_o), dtype=f32)
    perm = np.concatenate([np.arange(0, HD, 2), np.arange(1, HD, 2)])
    wqkv_p = np.array(wqkv, dtype=f32, copy=True)
    for h in range(HEADS):
        wqkv_p[:, h * HD:(h + 1) * HD] = wqkv_p[:, h * HD + perm]
    for g in range(KV):
        base = D + g * HD
        wqkv_p[:, base:base + HD] = wqkv_p[:, base + perm]
    wqkv_p = np.ascontiguousarray(wqkv_p)

    A = np.asarray(A, dtype=f32); Bm = np.asarray(Bm, dtype=f32); Wr = np.asarray(Wr, dtype=f32)
    AWr = np.concatenate([A.reshape(ER, D).T, Wr.T], axis=1).astype(f32)
    awr = np.ascontiguousarray(AWr.reshape(DT, 128, ER + NE))
    ballT = np.ascontiguousarray(Bm.transpose(0, 2, 1).reshape(ER, D), dtype=f32)
    bqT = np.ascontiguousarray(np.asarray(bq).reshape(DT, 128).T, dtype=f32)
    bkT = np.ascontiguousarray(np.asarray(bk).reshape(KV, 128).T, dtype=f32)
    bvT = np.ascontiguousarray(np.asarray(bv).reshape(KV, 128).T, dtype=f32)
    bo_bc = np.ascontiguousarray(np.broadcast_to(np.asarray(bo, dtype=f32)[None, :], (128, D)))
    identb = np.eye(128, dtype=ml_dtypes.bfloat16)
    ident32 = np.eye(128, dtype=f32)
    e4m = np.zeros((NE, ER), dtype=ml_dtypes.bfloat16)
    for e in range(NE):
        e4m[e, e * LR:(e + 1) * LR] = 1
    onesb = np.ones((128, 1), dtype=ml_dtypes.bfloat16)
    import math as _m
    fconst = np.ascontiguousarray(np.broadcast_to(np.array([1e-5, -0.5, -1.0, 1.0 / _m.sqrt(HD), 1.0, 0.0, SCALE_FFN, 1e-10], dtype=f32)[None, :], (128, 8)))

    shared = dict(wqT=wqT, wkT=wkT, wvT=wvT, woT=woT, wqkv=wqkv_p, wo2=wo2,
                  awr=awr, ballT=ballT, bqT=bqT, bkT=bkT, bvT=bvT, bo_bc=bo_bc,
                  identb=identb, ident32=ident32, e4=e4m, onesb=onesb,
                  fconst=fconst)

    inv_freq = 10000.0 ** (-(np.arange(0, HD, 2, dtype=np.float64) / HD))
    token_arr = np.full((1, 8), np.random.randint(1, 2**31 - 2, dtype=np.int64),
                        np.int32)
    in_maps, perms = [], []
    for c in range(2 * B):
        b, r = c // 2, c % 2
        tiles = np.arange(nt) * 2 + r
        tok = (tiles[:, None] * 128 + np.arange(128)[None, :]).reshape(-1)
        perms.append((b, tok))
        x0 = np.ascontiguousarray(np.asarray(hs[b], dtype=f32)[tok, :])
        ang = tok[None, :].astype(np.float64) * inv_freq[:, None]
        im = dict(shared)
        maskm = np.zeros((2, 128, 512), dtype=f32)
        col = np.arange(512)[None, :]
        row = np.arange(128)[:, None]
        for m in range(2):
            q = 2 * m + r
            maskm[m] = np.where(col <= 128 * q + row, 0.0, NEG).astype(f32)
        im.update(x0=x0, cosf=np.ascontiguousarray(np.cos(ang).astype(f32)),
                  sinf=np.ascontiguousarray(np.sin(ang).astype(f32)), maskd=maskm,
                  rk=np.full((1, 8), r, np.int32), tokin=token_arr)
        in_maps.append(im)
    return in_maps, perms


def kernel(**inputs):
    nt = 16
    in_maps, perms = _prep_inputs(inputs, nt=nt)
    nc = build(nt=nt)
    res = run_bass_kernel_spmd(nc, in_maps, core_ids=list(range(8)))
    hs = inputs["L0_attn_a_hidden_states"]
    B, SEQ, _ = hs.shape
    outp = np.empty((B, SEQ, D), dtype=np.float32)
    for c, (b, tok) in enumerate(perms):
        outp[b, tok, :] = res.results[c]["out"]
    return outp
